# version 06
# speedup vs baseline: 1.8406x; 1.1017x over previous
import os
import numpy as np
import ml_dtypes
from contextlib import ExitStack
import concourse.bass as bass
import concourse.mybir as mybir

F32 = mybir.dt.float32
BF16 = mybir.dt.bfloat16
I32 = mybir.dt.int32
AF = mybir.ActivationFunctionType
ALU = mybir.AluOpType
AX = mybir.AxisListType.X

NT, D, ND, H, HD, T = 16, 1024, 8, 16, 64, 2048
MAGIC = 12582912.0
AFR = os.environ.get("KB_AFR", "")


class B:
    def __init__(self, nc, ctx, blk):
        self.nc, self.ctx, self.blk = nc, ctx, blk
        self.q = {"tensor": [], "vector": [], "scalar": [], "gpsimd": []}
        self.autoflush = bool(int(os.environ.get("KB_AUTOFLUSH", "0")))
        self.defer_eng = set(os.environ.get(
            "KB_DEFER_ENG", "tensor,scalar,gpsimd").split(","))
        self.cnt = {}
        self.sem = {}
        for name in ("dl", "dg", "ds", "do", "T", "V", "S", "G"):
            self.sem[name] = ctx.enter_context(nc.semaphore(name))
            self.cnt[name] = 0

    def sbuf(self, name, shape, dtype, stack=None):
        return (stack or self.ctx).enter_context(
            self.nc.sbuf_tensor(name, shape, dtype))

    def psum(self, name, shape, dtype):
        return self.ctx.enter_context(self.nc.psum_tensor(name, shape, dtype))

    def _step(self, attr, semname, f):
        semh = self.sem[semname]

        def g(h):
            last = f(h)
            assert last is not None, "step fn must return its last instruction"
            last.then_inc(semh, 1)
        self.q[attr].append(g)
        self.cnt[semname] += 1
        if self.autoflush or attr not in self.defer_eng:
            self.flush_one(attr)
        return self.cnt[semname]

    def flush_one(self, attr):
        ops = self.q[attr]
        if ops:
            if os.environ.get("KB_FLOG"):
                print("flush %s: %d steps" % (attr, len(ops)))
            getattr(self.blk, attr)(
                lambda h, ops=tuple(ops): [f(h) for f in ops])
            self.q[attr] = []

    def flush(self):
        for attr in ("tensor", "vector", "scalar", "gpsimd"):
            self.flush_one(attr)

    def stepT(self, f):
        return self._step("tensor", "T", f)

    def stepV(self, f):
        return self._step("vector", "V", f)

    def stepS(self, f):
        return self._step("scalar", "S", f)

    def stepG(self, f):
        return self._step("gpsimd", "G", f)

    def dma(self, eng, semname, dst, src, waits=(), slow=False):
        sm = self.sem[semname]
        allsem = self.sem

        def g(e):
            for sn, vv in waits:
                e.wait_ge(allsem[sn], vv)
            e.dma_start(dst, src,
                        allow_slow_non_contiguous=slow).then_inc(sm, 16)
        getattr(self.blk, eng)(g)
        self.cnt[semname] += 16
        return self.cnt[semname]

    def w(self, h, *pairs):
        for sn, vv in pairs:
            h.wait_ge(self.sem[sn], vv)

    def barrier(self):
        self.flush()
        snap = {k: self.cnt[k] for k in ("T", "V", "S", "G", "dl", "dg", "ds")}
        for attr, semname in (("tensor", "T"), ("vector", "V"), ("scalar", "S"),
                              ("gpsimd", "G")):
            names = [n for n in snap if n != semname]

            def g(h, names=names, semname=semname, attr=attr):
                for n in names:
                    if snap[n]:
                        h.wait_ge(self.sem[n], snap[n])
                if attr == "vector":
                    last = h.memset(self.bscratch[0:1, 0:4], 0.0)
                elif attr == "scalar":
                    last = h.copy(self.bscratch[0:1, 4:8], self.bscratch[0:1, 0:4])
                else:
                    last = h.engine_nop() if hasattr(h, "engine_nop") else h.nop()
                last.then_inc(self.sem[semname], 1)
            getattr(self.blk, attr)(g)
            self.cnt[semname] += 1

        def gs(e):
            for n in ("T", "V", "S", "G"):
                e.wait_ge(self.sem[n], self.cnt[n])
        self.blk.sync(gs)


def build(nc, x_ap, w, out_ap, n_stages=4, stop=None):
    with ExitStack() as ctx, nc.Block() as blk:
        b = B(nc, ctx, blk)
        b.bscratch = b.sbuf("bscr", [1, 8], F32)
        emit(b, x_ap, w, out_ap, n_stages, stop)
    return nc


def emit(b, x_ap, w, out_ap, n_stages, stop=None):
    class _Stop(Exception):
        pass

    def ck(name):
        if stop == name:
            raise _Stop()

    sem = b.sem

    xbuf = b.sbuf("xbuf", [128, NT * D], F32)
    fmA = b.sbuf("fmA", [128, ND * T], BF16)
    wsb = b.sbuf("wsb", [128, ND * 512], BF16)
    btile = b.sbuf("btile", [128, 512], F32)
    xcb = b.sbuf("xcb", [128, D], BF16)
    cmagic = b.sbuf("cmagic", [128, 1], F32)
    cEPSA = b.sbuf("cEPSA", [128, 1], F32)
    cEPSL = b.sbuf("cEPSL", [128, 1], F32)
    cM16 = b.sbuf("cM16", [128, 1], F32)
    ident = b.sbuf("ident", [128, 128], BF16)
    identf = b.sbuf("identf", [128, 128], F32)
    onespad = b.sbuf("onespad", [128, 128], BF16)
    st1 = b.sbuf("st1", [128, 16], F32)
    st2 = b.sbuf("st2", [128, 16], F32)
    st3 = b.sbuf("st3", [128, 16], F32)
    qch = b.sbuf("qch", [128, 512], F32)
    qsq = b.sbuf("qsq", [128, 512], F32)
    qnb = b.sbuf("qnb", [128, 512], BF16)

    pT = b.psum("pT", [128, 1024], BF16)
    pS1 = b.psum("pS1", [128, 512], F32)
    pW = b.psum("pW", [128, 1024], F32)
    pBig = b.psum("pBig", [128, 2048], F32)

    xbuf_c = xbuf[:].rearrange("p (c f) -> p c f", c=NT)

    b.dma("sync", "dl", ident[:], w["cident"])
    b.dma("sync", "dl", xbuf_c, x_ap.rearrange("(c p) f -> p c f", p=128))
    dl_init = b.cnt["dl"]

    def i_v(v):
        b.w(v, ("dl", 16))
        v.tensor_copy(identf[:], ident[:])
        v.memset(cmagic[:], 0.5)
        v.drain()
        v.tensor_scalar_add(cmagic[:], cmagic[:], MAGIC)
        v.drain()
        v.tensor_scalar_sub(cmagic[:], cmagic[:], MAGIC)
        v.drain()
        v.tensor_scalar(cmagic[:], cmagic[:], 2.0 * (1.5 * 2.0 ** 52 - MAGIC), MAGIC,
                        ALU.mult, ALU.add)
        v.memset(onespad[:], 0.0)
        v.drain()
        v.memset(onespad[64:65, :], 1.0)
        v.memset(cEPSA[:], 64e-6)
        v.memset(cEPSL[:], 1e-5)
        return v.memset(cM16[:], -16.0)
    v_init = b.stepV(i_v)

    def cast_transpose(src_get, dst_fm, ndc=ND, waits=()):
        for tc in range(NT):
            tprev = b.cnt["T"]

            def cast(s, tc=tc, tprev=tprev):
                if tc == 0:
                    b.w(s, *waits)
                b.w(s, ("T", tprev))
                return s.activation(xcb[:, 0:ndc * 128], src_get(tc), AF.Copy)
            sv = b.stepS(cast)

            def tp(t, sv=sv):
                b.w(t, ("S", sv))
                for dc in range(ndc - 1):
                    t.transpose(pT[:, dc * 128:(dc + 1) * 128],
                                xcb[:, dc * 128:(dc + 1) * 128], ident[:])
                dc = ndc - 1
                return t.transpose(pT[:, dc * 128:(dc + 1) * 128],
                                   xcb[:, dc * 128:(dc + 1) * 128], ident[:])
            tv = b.stepT(tp)

            def evac(s, tc=tc, tv=tv):
                b.w(s, ("T", tv))
                for dc in range(ndc - 1):
                    s.copy(dst_fm[:, dc * T + tc * 128:dc * T + (tc + 1) * 128],
                           pT[:, dc * 128:(dc + 1) * 128])
                dc = ndc - 1
                return s.copy(dst_fm[:, dc * T + tc * 128:dc * T + (tc + 1) * 128],
                              pT[:, dc * 128:(dc + 1) * 128])
            b.stepS(evac)

    def load_w_strip(wkey, col0, ncols, dst=None, nrows=D, row0c=0, waits=()):
        dst = wsb if dst is None else dst
        nrc = nrows // 128
        return b.dma("gpsimd", "dg",
                     dst[:, 0:nrc * ncols].rearrange("p (c f) -> p c f", c=nrc),
                     w[wkey].rearrange("(c p) f -> p c f", p=128)
                     [:, row0c:row0c + nrc, col0:col0 + ncols],
                     waits=waits)

    def load_btile(bkey, col0, ncols, waits=()):
        return b.dma("gpsimd", "dg", btile[:, 0:ncols],
                     w[bkey][None, col0:col0 + ncols].broadcast_to([128, ncols]),
                     waits=waits)

    def stream_ln(buf):
        for tc in range(NT):
            ch = buf[:].rearrange("p (c f) -> p c f", c=NT)[:, tc, :]

            def stats(v, ch=ch):
                v.bn_stats(st1[:, 0:6], ch[:, 0:512])
                v.bn_stats(st1[:, 6:12], ch[:, 512:1024])
                v.drain()
                return v.bn_aggr(st2[:, 0:2], st1[:, 0:12])
            vv = b.stepV(stats)

            def sq(s, vv=vv):
                b.w(s, ("V", vv))
                return s.activation(st2[:, 2:3], st2[:, 1:2], AF.Sqrt, bias=cEPSL[:])
            sv = b.stepS(sq)

            def ab(v, sv=sv):
                b.w(v, ("S", sv))
                v.reciprocal(st2[:, 3:4], st2[:, 2:3])
                v.drain()
                v.tensor_scalar(st2[:, 4:5], st2[:, 0:1], st2[:, 3:4], -1.0,
                                ALU.mult, ALU.mult)
                v.drain()
                return v.tensor_copy(st2[:, 5:6], st2[:, 4:5])
            vv2 = b.stepV(ab)

            def apply(v, ch=ch, vv2=vv2):
                b.w(v, ("V", vv2))
                v.drain()
                return v.tensor_scalar(ch, ch, st2[:, 3:4], st2[:, 5:6],
                                       ALU.mult, ALU.add)
            b.stepV(apply)

    try:
        ck("init")
        cast_transpose(lambda tc: xbuf_c[:, tc, :], fmA,
                       waits=(("dl", dl_init), ("V", v_init)))
        fmA_done = b.cnt["S"]
        ck("fmA")

        s1ctx = ExitStack()
        GFm = b.sbuf("GFm", [16, T], BF16, s1ctx)
        bqg = b.sbuf("bqg", [16, 1], F32, s1ctx)
        dv_g = load_w_strip("L0_attn_Wq", D, 16)
        dv_b = b.dma("gpsimd", "dg", bqg[:], w["L0_attn_bq"][D:D + 16][:, None])

        def mm_acc(t, dst, lhs_of, rhs_of, n):
            for i in range(n - 1):
                t.matmul(dst, lhs_of(i), rhs_of(i), start=(i == 0), stop=False)
            return t.matmul(dst, lhs_of(n - 1), rhs_of(n - 1),
                            start=(n == 1), stop=True)

        for s4 in range(4):
            sprev = b.cnt["S"]

            def gmm(t, s4=s4, sprev=sprev):
                b.w(t, ("dg", dv_g), ("S", fmA_done))
                if s4 > 0:
                    b.w(t, ("S", sprev))
                return mm_acc(t, pS1[0:16, 0:512],
                              lambda dc: wsb[:, dc * 16:(dc + 1) * 16],
                              lambda dc, s4=s4: fmA[:, dc * T + s4 * 512:dc * T + (s4 + 1) * 512],
                              ND)
            tv = b.stepT(gmm)

            def gevac(s, s4=s4, tv=tv):
                b.w(s, ("T", tv), ("dg", dv_b))
                return s.activation(GFm[:, s4 * 512:(s4 + 1) * 512],
                                    pS1[0:16, 0:512], AF.Sigmoid, bias=bqg[:])
            b.stepS(gevac)
        ck("gate")

        QnF = b.sbuf("QnF", [128, 4 * T], BF16, s1ctx)
        KnF = b.sbuf("KnF", [128, 4 * T], BF16, s1ctx)
        Vp = b.sbuf("Vp", [128, NT * 520], BF16, s1ctx)
        ETb = b.sbuf("ETb", [128, 2 * 1024], BF16, s1ctx)
        stgQ = b.sbuf("stgQ", [128, T], BF16, s1ctx)
        stgQB = wsb[:, 0:T]
        stgY = b.sbuf("stgY", [65, T], BF16, s1ctx)
        NW = b.sbuf("NW", [128, 512], F32, s1ctx)
        glr = b.sbuf("glr", [65, T], BF16, s1ctx)
        dnr = b.sbuf("dnr", [65, 1024], F32, s1ctx)
        bcst = b.sbuf("bcst", [128, 1024], BF16, s1ctx)
        yTh = b.sbuf("yTh", [128, 4 * T], BF16, s1ctx)
        Vp_c = Vp[:].rearrange("p (c h f) -> p c h f", c=NT, h=8)

        def zero_misc(v):
            b.w(v, ("V", v_init))
            v.memset(bcst[:], 0.0)
            v.memset(stgQ[:], 0.0)
            return v.memset(Vp_c[:, :, :, 64:65], 1.0)
        v_misc = b.stepV(zero_misc)
        gscr = b.nc.dram_tensor("g_scr", [1024], BF16, kind="Internal").ap()
        HST = {"exp": [], "av": [], "smm_tv": [], "smm_last": [],
               "gs_last": [], "gev_last": [], "gw_last": [], "gh": 0}

        def l0_proj_norm(half, wkey, bkey, nwkey, nwscale, dst_fm):
            col0 = half * 512
            wdv = load_w_strip(wkey, col0, 512, waits=[("T", b.cnt["T"])])
            bdv = load_btile(bkey, col0, 512, waits=[("V", b.cnt["V"])])
            nwdv = b.dma("gpsimd", "dg", NW[:],
                         w[nwkey][None, None, :].broadcast_to([128, 8, 64]),
                         waits=[("V", b.cnt["V"])])

            def nw_scale(v):
                b.w(v, ("dg", nwdv))
                return v.tensor_scalar_mul(NW[:], NW[:], nwscale)
            nw_vv = b.stepV(nw_scale)

            for tc in range(NT):
                prev_v = b.cnt["V"]

                def pmm(t, tc=tc, prev_v=prev_v):
                    b.w(t, ("dg", wdv), ("S", fmA_done), ("V", prev_v))
                    return mm_acc(t, pW[:, 0:512],
                                  lambda dc, tc=tc: fmA[:, dc * T + tc * 128:dc * T + (tc + 1) * 128],
                                  lambda dc: wsb[:, dc * 512:(dc + 1) * 512], ND)
                tv = b.stepT(pmm)

                def ev1(v, tv=tv):
                    b.w(v, ("T", tv), ("dg", bdv), ("V", nw_vv))
                    return v.tensor_tensor(qch[:], pW[:, 0:512], btile[:, 0:512],
                                           ALU.add)
                vv1 = b.stepV(ev1)

                def sq(s, vv1=vv1):
                    b.w(s, ("V", vv1))
                    return s.activation(qsq[:], qch[:], AF.Square)
                sv1 = b.stepS(sq)

                def red(v, sv1=sv1):
                    b.w(v, ("S", sv1))
                    for hh in range(7):
                        v.tensor_reduce(st1[:, hh:hh + 1],
                                        qsq[:, hh * 64:(hh + 1) * 64],
                                        axis=AX, op=ALU.add)
                    return v.tensor_reduce(st1[:, 7:8], qsq[:, 7 * 64:8 * 64],
                                           axis=AX, op=ALU.add)
                vv2 = b.stepV(red)

                def rsq(s, vv2=vv2):
                    b.w(s, ("V", vv2))
                    return s.activation(st2[:, 0:8], st1[:, 0:8], AF.Sqrt,
                                        bias=cEPSA[:])
                sv2 = b.stepS(rsq)

                def nrm(v, sv2=sv2):
                    b.w(v, ("S", sv2))
                    v.reciprocal(st3[:, 0:8], st2[:, 0:8])
                    v.drain()
                    v.tensor_copy(qsq[:].rearrange("p (h f) -> p h f", h=8),
                                  st3[:, 0:8][:, :, None].broadcast_to([128, 8, 64]))
                    v.drain()
                    v.tensor_mul(qsq[:], qsq[:], NW[:])
                    v.drain()
                    return v.tensor_mul(qnb[:], qch[:], qsq[:])
                vv3 = b.stepV(nrm)

                def tp(t, vv3=vv3, sprev=b.cnt["S"]):
                    b.w(t, ("V", vv3), ("S", sprev))
                    for dc in range(3):
                        t.transpose(pT[:, dc * 128:(dc + 1) * 128],
                                    qnb[:, dc * 128:(dc + 1) * 128], ident[:])
                    return t.transpose(pT[:, 3 * 128:4 * 128],
                                       qnb[:, 3 * 128:4 * 128], ident[:])
                tv2 = b.stepT(tp)

                def ev2(s, tc=tc, tv2=tv2, dst_fm=dst_fm):
                    b.w(s, ("T", tv2))
                    for dc in range(3):
                        s.copy(dst_fm[:, dc * T + tc * 128:dc * T + (tc + 1) * 128],
                               pT[:, dc * 128:(dc + 1) * 128])
                    return s.copy(dst_fm[:, 3 * T + tc * 128:3 * T + (tc + 1) * 128],
                                  pT[:, 3 * 128:4 * 128])
                b.stepS(ev2)

        def l0_vproj(half):
            col0 = half * 512
            wdv = load_w_strip("L0_attn_Wv", col0, 512, waits=[("T", b.cnt["T"])])
            bdv = load_btile("L0_attn_bv", col0, 512, waits=[("V", b.cnt["V"])])
            for tc in range(NT):
                prev_v = b.cnt["V"]

                def pmm(t, tc=tc, prev_v=prev_v):
                    b.w(t, ("dg", wdv), ("S", fmA_done), ("V", prev_v))
                    return mm_acc(t, pW[:, 0:512],
                                  lambda dc, tc=tc: fmA[:, dc * T + tc * 128:dc * T + (tc + 1) * 128],
                                  lambda dc: wsb[:, dc * 512:(dc + 1) * 512], ND)
                tv = b.stepT(pmm)

                def ev(v, tc=tc, tv=tv):
                    b.w(v, ("T", tv), ("dg", bdv), ("V", v_misc))
                    return v.scalar_tensor_tensor(
                        Vp_c[:, tc, :, 0:64],
                        pW[:, 0:512].rearrange("p (h f) -> p h f", h=8),
                        1.0,
                        btile[:, 0:512].rearrange("p (h f) -> p h f", h=8),
                        ALU.mult, ALU.add)
                b.stepV(ev)

        def l0_heads(half):
            def zB(v, tprev=b.cnt["T"]):
                b.w(v, ("T", tprev))
                return v.memset(stgQB[0:64, :], 0.0)
            b.stepV(zB)
            for hh in range(8):
                h = half * 8 + hh
                par = h % 2
                pair = hh // 2
                gh = HST["gh"]
                stq = stgQ if par == 0 else stgQB
                dsv = b.dma("sync", "ds",
                            stq[par * 64:par * 64 + 64, :],
                            QnF[par * 64:par * 64 + 64, pair * T:(pair + 1) * T],
                            waits=([("T", HST["smm_last"][gh - 2])] if hh >= 2
                                   else [("S", b.cnt["S"]), ("T", b.cnt["T"]),
                                         ("V", b.cnt["V"])]))
                gdv = b.dma("sync", "ds", glr[64:65, :], GFm[h:h + 1, :],
                            waits=([("V", HST["gs_last"][2 * gh - 1])] if gh >= 1
                                   else [("V", b.cnt["V"])]))

                for qh in range(2):
                    q0 = qh * 1024
                    pend_av = None
                    for kc in range(NT):
                        i = len(HST["exp"])
                        pb = pBig[:, 0:1024] if i % 2 == 0 else pBig[:, 1024:2048]
                        eth = ETb[:, 0:1024] if i % 2 == 0 else ETb[:, 1024:2048]

                        def smm(t, kc=kc, pb=pb, dsv=dsv, pair=pair, q0=q0,
                                stq=stq,
                                ex2=(HST["exp"][i - 2] if i >= 2 else None)):
                            if kc == 0:
                                b.w(t, ("ds", dsv))
                            if ex2 is not None:
                                b.w(t, ("S", ex2))
                            t.matmul(pb[:, 0:512],
                                     KnF[:, pair * T + kc * 128:pair * T + (kc + 1) * 128],
                                     stq[:, q0:q0 + 512], start=True, stop=True)
                            return t.matmul(pb[:, 512:1024],
                                            KnF[:, pair * T + kc * 128:pair * T + (kc + 1) * 128],
                                            stq[:, q0 + 512:q0 + 1024],
                                            start=True, stop=True)
                        tv = b.stepT(smm)
                        HST["smm_tv"].append(tv)

                        def expv(s, tv=tv, pb=pb, eth=eth,
                                 av2=(HST["av"][i - 2] if i >= 2 else None)):
                            b.w(s, ("T", tv))
                            if av2 is not None:
                                b.w(s, ("T", av2))
                            return s.activation(eth, pb, AF.Exp, bias=cM16[:])
                        sv = b.stepS(expv)
                        HST["exp"].append(sv)

                        def mk_avmm(kc, sv, eth, hh=hh):
                            def avmm(t, hh=hh, kc=kc, sv=sv, eth=eth,
                                     gv_prev=(HST["gev_last"][-1]
                                              if (kc == 0 and HST["gev_last"])
                                              else None)):
                                b.w(t, ("S", sv))
                                if gv_prev is not None:
                                    b.w(t, ("V", gv_prev))
                                t.matmul(pW[0:65, 0:512], Vp_c[:, kc, hh, :],
                                         eth[:, 0:512],
                                         start=(kc == 0), stop=False)
                                return t.matmul(pW[0:65, 512:1024],
                                                Vp_c[:, kc, hh, :],
                                                eth[:, 512:1024],
                                                start=(kc == 0),
                                                stop=(kc == NT - 1))
                            HST["av"].append(b.stepT(avmm))

                        if pend_av is not None:
                            mk_avmm(*pend_av)
                        pend_av = (kc, sv, eth)
                    mk_avmm(*pend_av)

                    def den_ev(s, avt=HST["av"][-1],
                               gs_prev=(HST["gs_last"][-1] if HST["gs_last"]
                                        else None)):
                        b.w(s, ("T", avt))
                        if gs_prev is not None:
                            b.w(s, ("V", gs_prev))
                        return s.copy(dnr[64:65, 0:1024], pW[64:65, 0:1024])
                    sv_d = b.stepS(den_ev)

                    def gs_mk(v, q0=q0, sv_d=sv_d, gdv=gdv,
                              gw_prev=(HST["gw_last"][-1] if HST["gw_last"]
                                       else None)):
                        b.w(v, ("S", sv_d), ("ds", gdv))
                        if gw_prev is not None:
                            b.w(v, ("ds", gw_prev))
                        v.reciprocal(dnr[64:65, 0:1024], dnr[64:65, 0:1024])
                        v.drain()
                        return v.tensor_tensor(bcst[64:65, 0:1024],
                                               dnr[64:65, 0:1024],
                                               glr[64:65, q0:q0 + 1024], ALU.mult)
                    vv_gs = b.stepV(gs_mk)
                    HST["gs_last"].append(vv_gs)

                    HST["gw_last"].append(
                        b.dma("sync", "ds", gscr[None, :], bcst[64:65, 0:1024],
                              waits=[("V", vv_gs)]))
                    bcb = b.dma("sync", "ds", bcst[0:64, 0:1024],
                                gscr[None, :].broadcast_to([64, 1024]),
                                waits=([("V", HST["gev_last"][-1])]
                                       if HST["gev_last"] else []))

                    for s2 in range(2):
                        c0 = q0 + s2 * 512

                        def gev(v, c0=c0, s2=s2, bcb=bcb):
                            b.w(v, ("ds", bcb))
                            return v.tensor_mul(stgY[0:64, c0:c0 + 512],
                                                pW[0:64, s2 * 512:(s2 + 1) * 512],
                                                bcst[0:64, s2 * 512:(s2 + 1) * 512])
                        HST["gev_last"].append(b.stepV(gev))

                ydv = b.dma("sync", "ds",
                            yTh[par * 64:par * 64 + 64, (hh // 2) * T:(hh // 2 + 1) * T],
                            stgY[0:64, :],
                            waits=[("V", b.cnt["V"])])
                HST["smm_last"].append(HST["smm_tv"][-1])
                HST["gh"] += 1
            return b.cnt["ds"]

        for half in range(2):
            l0_proj_norm(half, "L0_attn_Wq", "L0_attn_bq", "L0_attn_qn_w", 1.0, QnF)
            l0_proj_norm(half, "L0_attn_Wk", "L0_attn_bk", "L0_attn_kn_w", 8.0, KnF)
            ck("proj%d" % half)
            l0_vproj(half)
            ck("vproj%d" % half)
            yT_dma = l0_heads(half)
            ck("heads%d" % half)

            for strip in range(2):
                c0 = strip * 512
                wdv = load_w_strip("L0_attn_Wo", c0, 512, row0c=4 * half, nrows=512,
                                   waits=[("T", b.cnt["T"])])
                bdv = load_btile("L0_attn_bo", c0, 512, waits=[("V", b.cnt["V"])])
                for tc in range(NT):
                    po = pW[:, 0:512] if tc % 2 == 0 else pW[:, 512:1024]
                    prev_v = b.cnt["V"]

                    def omm(t, tc=tc, po=po, wdv=wdv, prev_v=prev_v, yT_dma=yT_dma):
                        b.w(t, ("dg", wdv), ("ds", yT_dma), ("V", prev_v))
                        return mm_acc(t, po,
                                      lambda dc, tc=tc: yTh[:, dc * T + tc * 128:dc * T + (tc + 1) * 128],
                                      lambda dc: wsb[:, dc * 512:(dc + 1) * 512], 4)
                    tv = b.stepT(omm)

                    def oev(v, tc=tc, tv=tv, po=po, c0=c0, bdv=bdv, half=half):
                        b.w(v, ("T", tv), ("dg", bdv))
                        xs = xbuf_c[:, tc, c0:c0 + 512]
                        if half == 0:
                            v.tensor_tensor(qch[:], po, btile[:, 0:512], ALU.add)
                            v.drain()
                            return v.tensor_tensor(xs, xs, qch[:], ALU.add)
                        else:
                            return v.tensor_tensor(xs, xs, po, ALU.add)
                    b.stepV(oev)

        ck("outproj")
        stream_ln(xbuf)
        ck("stage1")

        if n_stages >= 2:
            b.barrier()
            s1ctx.close()
            s1ctx = ExitStack()
            emit_l0_ffn(b, w, ck, dict(
                xbuf=xbuf, xbuf_c=xbuf_c, fmA=fmA, wsb=wsb, btile=btile,
                xcb=xcb, cmagic=cmagic, ident=ident, identf=identf, onespad=onespad,
                st1=st1, st2=st2, st3=st3, qch=qch, qsq=qsq, qnb=qnb,
                pT=pT, pS1=pS1, pW=pW, pBig=pBig, cEPSL=cEPSL,
                stream_ln=stream_ln, mm_acc=mm_acc))
        ck("stage2")

        if n_stages >= 3:
            b.barrier()
            emit_l1_attn(b, w, ck, dict(
                xbuf=xbuf, xbuf_c=xbuf_c, fmA=fmA, wsb=wsb, btile=btile,
                xcb=xcb, cmagic=cmagic, ident=ident, identf=identf,
                onespad=onespad, st1=st1, st2=st2, st3=st3, qch=qch, qsq=qsq,
                qnb=qnb, pT=pT, pS1=pS1, pW=pW, pBig=pBig, cEPSL=cEPSL,
                stream_ln=stream_ln, mm_acc=mm_acc))
        ck("stage3")

        if n_stages >= 4:
            b.barrier()
            emit_l1_ffn(b, w, ck, dict(
                xbuf=xbuf, xbuf_c=xbuf_c, fmA=fmA, wsb=wsb, btile=btile,
                xcb=xcb, ident=ident, st1=st1, st2=st2, qch=qch,
                pT=pT, pS1=pS1, pW=pW, pBig=pBig,
                stream_ln=stream_ln, mm_acc=mm_acc,
                cast_transpose=cast_transpose))
        ck("stage4")
    except _Stop:
        pass

    b.flush()
    s1ctx.close()
    vdone = b.cnt["V"]
    sdone = b.cnt["S"]
    odv = b.dma("sync", "do",
                out_ap.rearrange("(c p) f -> p c f", p=128), xbuf_c,
                waits=[("V", vdone), ("S", sdone)])

    def fin(e):
        e.wait_ge(sem["do"], odv)
    b.blk.sync(fin)


def emit_l0_ffn(b, w, ck, env):
    sem = b.sem
    nc = b.nc
    xbuf_c = env["xbuf_c"]
    fmA, wsb, xcb = env["fmA"], env["wsb"], env["xcb"]
    cmagic, ident, ones128 = env["cmagic"], env["ident"], env["onespad"]
    identf = env["identf"]
    st1 = env["st1"]
    qch, qsq = env["qch"], env["qsq"]
    pT, pS1, pW, pBig = env["pT"], env["pS1"], env["pW"], env["pBig"]
    stream_ln, mm_acc = env["stream_ln"], env["mm_acc"]
    xbuf = env["xbuf"]

    NG, TG = 8, 256
    NF = 32

    DBG = False
    wkind = "ExternalOutput" if DBG else "Internal"
    wq1t = nc.dram_tensor("wq1t_scr", [NF, 128, 1024], BF16, kind=wkind).ap()
    wq2t = nc.dram_tensor("wq2t_scr", [NF, 128, 1024], BF16, kind=wkind).ap()
    sclh = nc.dram_tensor("scl_scr", [8], F32, kind="Internal").ap()

    with ExitStack() as sc:
        ttile = b.sbuf("ttile", [128, 2 * 1024], BF16, sc)
        ttile2 = b.sbuf("ttile2", [128, 2 * 1024], BF16, sc)
        macc = b.sbuf("macc", [128, 32], F32, sc)
        mrow = b.sbuf("mrow", [1, 128], F32, sc)
        mtv = b.sbuf("mtv", [1, 8], F32, sc)
        m1c = b.sbuf("m1c", [128, 1], F32, sc)
        thrc = b.sbuf("thrc", [128, 1], F32, sc)
        m2d = b.sbuf("m2d", [128, 1], F32, sc)
        xfs = b.sbuf("xfs", [128, 1024], F32, sc)
        dqt = b.sbuf("dqt", [128, 16], F32, sc)
        dq16 = b.sbuf("dq16", [16, 128], F32, sc)
        dqpad = b.sbuf("dqpad", [128, T], BF16, sc)
        Br = b.sbuf("Br", [128, T], BF16, sc)
        cw = b.sbuf("cw", [128, 32 * 3], F32, sc)
        cb = b.sbuf("cb", [128, 32], F32, sc)
        shset = b.sbuf("shset", [128, NF * TG], F32, sc)
        sqset = b.sbuf("sqset", [128, NF * TG], BF16, sc)
        hb = b.sbuf("hb", [128, 2 * 258], F32, sc)
        s2pad = b.sbuf("s2pad", [128, 256], BF16, sc)
        s2t = b.sbuf("s2t", [128, 2], F32, sc)
        s2r = b.sbuf("s2r", [2, 128], F32, sc)
        Bs2 = b.sbuf("Bs2", [128, 256], F32, sc)
        r2t = b.sbuf("r2t", [128, 2], F32, sc)
        mxt = b.sbuf("mxt", [128, 4], F32, sc)
        gmax = b.sbuf("gmax", [128, 256], F32, sc)
        gmin = b.sbuf("gmin", [128, 256], F32, sc)
        dbgm = b.sbuf("dbgm", [128, 164], F32, sc)
        wq1b = b.sbuf("wq1b", [128, 4 * 1024], BF16, sc)
        wq2b = b.sbuf("wq2b", [128, 2 * 1024], BF16, sc)

        w1ap = w["L0_ffn_w1"].rearrange("(c p) f -> p c f", p=128)
        w2ap = w["L0_ffn_w2"].rearrange("(c p) f -> p c f", p=128)

        def zero2(v):
            v.memset(dqpad[:], 0.0)
            v.memset(s2pad[:], 0.0)
            v.memset(hb[:], 0.0)
            return v.memset(macc[:], 0.0)
        vz = b.stepV(zero2)

        def mean_pass(ap_chunks, col_ofs, n):
            v_after = []
            for i in range(n):
                qtr = i % 4
                eng, sn = (("sync", "ds") if i % 2 == 0 else ("gpsimd", "dg"))
                dv = b.dma(eng, sn, shset[:, qtr * 1024:(qtr + 1) * 1024],
                           ap_chunks(i),
                           waits=([("V", v_after[i - 4])] if i >= 4
                                  else [("V", b.cnt["V"])]))

                def red(v, i=i, qtr=qtr, dv=dv, sn=sn):
                    b.w(v, (sn, dv))
                    return v.tensor_reduce(macc[:, (col_ofs + i) % 32:(col_ofs + i) % 32 + 1],
                                           shset[:, qtr * 1024:(qtr + 1) * 1024],
                                           axis=AX, op=ALU.add,
                                           apply_absolute_value=True)
                v_after.append(b.stepV(red))

        def finish_mean(dst_col_m, dst_col_thr, denom, want_thr):
            def r2_(v):
                v.drain()
                v.tensor_reduce(st1[:, 0:1], macc[:, 0:32], axis=AX, op=ALU.add)
                v.drain()
                return v.tensor_copy(st1[:, 1:2], st1[:, 0:1])
            vv = b.stepV(r2_)

            def tpm(t, vv=vv, sprev=b.cnt["S"]):
                b.w(t, ("V", vv), ("S", sprev))
                return t.transpose(pS1[0:1, 0:128], st1[:, 0:1], identf[:])
            tv = b.stepT(tpm)

            def evm(s, tv=tv):
                b.w(s, ("T", tv))
                return s.copy(mrow[:], pS1[0:1, 0:128])
            sv = b.stepS(evm)

            def fin(v, sv=sv, denom=denom):
                b.w(v, ("S", sv))
                v.tensor_reduce(mtv[0:1, 0:1], mrow[:], axis=AX, op=ALU.add)
                v.drain()
                v.tensor_scalar_mul(mtv[0:1, 0:1], mtv[0:1, 0:1], 1.0 / denom)
                v.drain()
                v.tensor_scalar_max(mtv[0:1, dst_col_m:dst_col_m + 1],
                                    mtv[0:1, 0:1], 1e-5)
                v.drain()
                if want_thr:
                    v.tensor_scalar_mul(mtv[0:1, dst_col_thr:dst_col_thr + 1],
                                        mtv[0:1, dst_col_m:dst_col_m + 1], 0.5)
                else:
                    v.tensor_scalar_mul(mtv[0:1, dst_col_thr:dst_col_thr + 1],
                                        mtv[0:1, dst_col_m:dst_col_m + 1], 1.0 / 127.0)
                v.drain()
                v.reciprocal(mtv[0:1, 6:7], mtv[0:1, dst_col_m:dst_col_m + 1])
                v.drain()
                return v.tensor_copy(mtv[0:1, 7:8], mtv[0:1, 0:1])
            return b.stepV(fin)

        mean_pass(lambda i: w1ap[:, i, :], 0, 32)
        vvm = finish_mean(1, 2, 4096.0 * 1024.0, True)

        def dbgsnap(v, vvm=vvm):
            b.w(v, ("V", vvm))
            v.tensor_copy(dbgm[:, 0:32], macc[:, 0:32])
            v.tensor_copy(dbgm[:, 32:33], st1[:, 0:1])
            return v.tensor_copy(dbgm[0:1, 33:161], mrow[:])
        b.stepV(dbgsnap)
        b.dma("sync", "ds", sclh[None, :], mtv[0:1, :], waits=[("V", vvm)])
        hb1 = b.cnt["ds"]
        bdv1 = b.dma("gpsimd", "dg", m1c[:],
                     sclh[None, 1:2].broadcast_to([128, 1]), waits=[("ds", hb1)])
        bdv2 = b.dma("gpsimd", "dg", thrc[:],
                     sclh[None, 6:7].broadcast_to([128, 1]), waits=[("ds", hb1)])

        def tern_pass(ap_of, n, scr, dst_of, scale_dv):
            v_after, s_after, d_spill = [], [], []
            for i in range(n):
                qtr = i % 4
                half = i % 2
                eng, sn = (("sync", "ds") if i % 2 == 0 else ("gpsimd", "dg"))
                dv = b.dma(eng, sn, shset[:, qtr * 1024:(qtr + 1) * 1024],
                           ap_of(i),
                           waits=([("V", v_after[i - 4])] if i >= 4
                                  else [("V", b.cnt["V"]), ("S", b.cnt["S"])]))
                tt = ttile[:, half * 1024:(half + 1) * 1024] if False else None

                def tern(v, qtr=qtr, half=half, dv=dv, sn=sn,
                         tprev=b.cnt["T"], scale_dv=scale_dv):
                    b.w(v, (sn, dv), ("dg", scale_dv), ("T", tprev))
                    v.tensor_scalar(xfs[:], shset[:, qtr * 1024:(qtr + 1) * 1024],
                                    thrc[:], -1.0, ALU.mult, ALU.max)
                    v.tensor_scalar(xfs[:], xfs[:], 1.0, cmagic[:],
                                    ALU.min, ALU.add)
                    return v.tensor_scalar(ttile[:, half * 1024:(half + 1) * 1024],
                                           xfs[:], cmagic[:], None,
                                           ALU.subtract)
                vv2 = b.stepV(tern)
                v_after.append(vv2)

                def tpp(t, vv2=vv2, half=half,
                        sprev=(s_after[i - 1] if i >= 1 else b.cnt["S"])):
                    b.w(t, ("V", vv2), ("S", sprev))
                    for dc in range(7):
                        t.transpose(pT[:, dc * 128:(dc + 1) * 128],
                                    ttile[:, half * 1024 + dc * 128:half * 1024 + (dc + 1) * 128],
                                    ident[:])
                    return t.transpose(pT[:, 7 * 128:8 * 128],
                                       ttile[:, half * 1024 + 7 * 128:half * 1024 + 8 * 128],
                                       ident[:])
                tv = b.stepT(tpp)

                def evt(s, tv=tv, half=half,
                        dprev=(d_spill[i - 2] if i >= 2 else b.cnt["ds"])):
                    b.w(s, ("T", tv), ("ds", dprev))
                    for dc in range(7):
                        s.copy(ttile2[:, half * 1024 + dc * 128:half * 1024 + (dc + 1) * 128],
                               pT[:, dc * 128:(dc + 1) * 128])
                    return s.copy(ttile2[:, half * 1024 + 7 * 128:half * 1024 + 8 * 128],
                                  pT[:, 7 * 128:8 * 128])
                sv2 = b.stepS(evt)
                s_after.append(sv2)
                d_spill.append(b.dma("sync", "ds", dst_of(i),
                                     ttile2[:, half * 1024:(half + 1) * 1024],
                                     waits=[("S", sv2)]))

        tern_pass(lambda i: w1ap[:, i, :], 32, wq1t, lambda i: wq1t[i], bdv2)

        def zero_macc(v, vprev=b.cnt["V"]):
            b.w(v, ("V", vprev))
            return v.memset(macc[:], 0.0)
        b.stepV(zero_macc)
        mean_pass(lambda i: w2ap[:, i // 4, (i % 4) * 1024:(i % 4 + 1) * 1024], 0, 32)
        vvm2 = finish_mean(3, 4, 1024.0 * 4096.0, False)
        def thr2(v, vvm2=vvm2):
            b.w(v, ("V", vvm2))
            return v.tensor_scalar_mul(mtv[0:1, 5:6], mtv[0:1, 3:4], 0.5)
        vvt2 = b.stepV(thr2)
        b.dma("sync", "ds", sclh[None, :], mtv[0:1, :],
              waits=[("V", vvt2), ("dg", b.cnt["dg"])])
        hb2 = b.cnt["ds"]
        bdv3 = b.dma("gpsimd", "dg", m2d[:],
                     sclh[None, 4:5].broadcast_to([128, 1]), waits=[("ds", hb2)])
        bdv4 = b.dma("gpsimd", "dg", thrc[:],
                     sclh[None, 6:7].broadcast_to([128, 1]),
                     waits=[("ds", hb2), ("V", b.cnt["V"])])
        tern_pass(lambda i: w2ap[:, i // 4, (i % 4) * 1024:(i % 4 + 1) * 1024],
                  32, wq2t,
                  lambda i: wq2t[:].rearrange("c p f -> p c f")[:, (i % 4) * 8:(i % 4) * 8 + 8,
                                                               (i // 4) * 128:(i // 4 + 1) * 128]
                  if False else
                  wq2t.rearrange("c p f -> p c f", c=NF)[:, (i % 4) * 8:(i % 4) * 8 + 8,
                                                         (i // 4) * 128:(i // 4 + 1) * 128],
                  bdv4)

        cdv = b.dma("sync", "ds", cw[:].rearrange("p (c t) -> p c t", c=32),
                    w["L0_ffn_conv_w"].rearrange("(c p) u t -> p c (u t)", p=128),
                    slow=True)
        cbdv = b.dma("sync", "ds", cb[:],
                     w["L0_ffn_conv_b"].rearrange("(c p) -> p c", p=128),
                     slow=True)

        def cwm_mk(v, cdv=cdv):
            b.w(v, ("ds", cdv), ("dg", bdv1))
            return v.tensor_scalar_mul(cw[:], cw[:], m1c[:])
        v_cwm = b.stepV(cwm_mk)

        for tc in range(NT):
            ch = xbuf_c[:, tc, :]
            tprev = b.cnt["T"]

            def q1(v, ch=ch, tc=tc, tprev=tprev):
                b.w(v, ("T", tprev))
                v.tensor_reduce(st1[:, 0:1], ch, axis=AX, op=ALU.max,
                                apply_absolute_value=True)
                v.drain()
                v.tensor_scalar_max(st1[:, 1:2], st1[:, 0:1], 1e-5)
                v.drain()
                v.tensor_scalar_mul(dqt[:, tc:tc + 1], st1[:, 1:2], 1.0 / 127.0)
                v.reciprocal(st1[:, 2:3], st1[:, 1:2])
                v.drain()
                v.tensor_scalar_mul(st1[:, 3:4], st1[:, 2:3], 127.0)
                v.drain()
                v.tensor_scalar(xfs[:], ch, st1[:, 3:4], cmagic[:], ALU.mult, ALU.add)
                v.drain()
                return v.tensor_scalar(xcb[:], xfs[:], cmagic[:], None, ALU.subtract)
            sv = b.stepV(q1)

            def tpq(t, sv=sv, sprev=b.cnt["S"]):
                b.w(t, ("V", sv), ("S", sprev))
                for dc in range(7):
                    t.transpose(pT[:, dc * 128:(dc + 1) * 128],
                                xcb[:, dc * 128:(dc + 1) * 128], ident[:])
                return t.transpose(pT[:, 7 * 128:8 * 128],
                                   xcb[:, 7 * 128:8 * 128], ident[:])
            tv = b.stepT(tpq)

            def evq(s, tc=tc, tv=tv):
                b.w(s, ("T", tv))
                for dc in range(7):
                    s.copy(fmA[:, dc * T + tc * 128:dc * T + (tc + 1) * 128],
                           pT[:, dc * 128:(dc + 1) * 128])
                return s.copy(fmA[:, 7 * T + tc * 128:7 * T + (tc + 1) * 128],
                              pT[:, 7 * 128:8 * 128])
            b.stepS(evq)


        def tpd(t, vprev=b.cnt["V"], sprev=b.cnt["S"]):
            b.w(t, ("V", vprev), ("S", sprev))
            return t.transpose(pS1[0:16, 0:128], dqt[:], identf[:])
        tv = b.stepT(tpd)

        def evd(s, tv=tv):
            b.w(s, ("T", tv))
            return s.copy(dq16[:], pS1[0:16, 0:128])
        sv = b.stepS(evd)
        for r in range(16):
            ddv = b.dma("gpsimd", "dg", dqpad[64:65, r * 128:(r + 1) * 128],
                        dq16[r:r + 1, :], waits=([("S", sv)] if r == 0 else []))
        for s4 in range(4):
            def brm(t, s4=s4, ddv=ddv, vprev=b.cnt["V"]):
                b.w(t, ("dg", ddv), ("V", vprev))
                return t.matmul(pS1[:, 0:512], ones128[:],
                                dqpad[:, s4 * 512:(s4 + 1) * 512],
                                start=True, stop=True)
            tvb = b.stepT(brm)

            def bre(v, s4=s4, tvb=tvb):
                b.w(v, ("T", tvb))
                return v.tensor_copy(Br[:, s4 * 512:(s4 + 1) * 512], pS1[:, 0:512])
            b.stepV(bre)

        g1hist, g2hist = [], []
        w2_spill_done = b.cnt["ds"]
        for tg in range(NG):
            t0 = tg * TG
            lo = max(t0 - 1, 0)
            hi = min(t0 + TG + 1, T)
            span = hi - lo
            hofs = 1 - (t0 - lo)
            if tg == NG - 1:
                def zg(v, vprev=b.cnt["V"]):
                    b.w(v, ("V", vprev))
                    v.memset(hb[:, 257:258], 0.0)
                    return v.memset(hb[:, 258 + 257:258 + 258], 0.0)
                b.stepV(zg)
            for dffc in range(NF):
                half = dffc % 2
                g1 = len(g1hist)
                qtr = g1 % 4
                hbv = hb[:, half * 258:half * 258 + 258]
                pmm1 = pS1[:, 0:258] if half == 0 else pW[:, 0:258]
                w1dv = b.dma("sync", "ds", wq1b[:, qtr * 1024:(qtr + 1) * 1024],
                             wq1t[dffc],
                             waits=[("T", g1hist[g1 - 4])] if g1 >= 4 else [])

                def m1f(t, tg=tg, dffc=dffc, qtr=qtr, w1dv=w1dv, lo=lo, span=span,
                        pmm1=pmm1, vprev=b.cnt["V"]):
                    b.w(t, ("ds", w1dv), ("V", vprev))
                    return mm_acc(t, pmm1[:, 0:span],
                                  lambda dc, qtr=qtr: wq1b[:, qtr * 1024 + dc * 128:
                                                           qtr * 1024 + (dc + 1) * 128],
                                  lambda dc, lo=lo, span=span: fmA[:, dc * T + lo:dc * T + lo + span],
                                  ND)
                tv1 = b.stepT(m1f)
                g1hist.append(tv1)

                def hev(v, tg=tg, half=half, tv1=tv1, pmm1=pmm1, lo=lo, span=span,
                        hofs=hofs, hbv=hbv):
                    b.w(v, ("T", tv1))
                    return v.tensor_tensor(hbv[:, hofs:hofs + span], pmm1[:, 0:span],
                                           Br[:, lo:lo + span], ALU.mult)
                vv1 = b.stepV(hev)

                def conv(v, hbv=hbv, dffc=dffc, vv1=vv1, sprev=b.cnt["S"]):
                    b.w(v, ("V", vv1), ("S", sprev))
                    v.tensor_scalar(qch[:, 0:256], hbv[:, 0:256],
                                    cw[:, dffc * 3:dffc * 3 + 1], None, ALU.mult)
                    v.drain()
                    v.scalar_tensor_tensor(qch[:, 0:256], hbv[:, 1:257],
                                           cw[:, dffc * 3 + 1:dffc * 3 + 2],
                                           qch[:, 0:256], ALU.mult, ALU.add)
                    v.drain()
                    return v.scalar_tensor_tensor(qch[:, 0:256], hbv[:, 2:258],
                                                  cw[:, dffc * 3 + 2:dffc * 3 + 3],
                                                  qch[:, 0:256], ALU.mult, ALU.add)
                vv2 = b.stepV(conv)

                def silu(s, dffc=dffc, vv2=vv2, cbdv=cbdv):
                    b.w(s, ("V", vv2), ("ds", cbdv))
                    return s.activation(shset[:, dffc * TG:(dffc + 1) * TG],
                                        qch[:, 0:256], AF.Silu,
                                        bias=cb[:, dffc:dffc + 1])
                b.stepS(silu)

            def gfold(v, sprev=b.cnt["S"]):
                b.w(v, ("S", sprev))
                for op, dst in ((ALU.max, gmax), (ALU.min, gmin)):
                    v.tensor_tensor(dst[:], shset[:, 0:256], shset[:, 256:512], op)
                    for i in range(2, NF):
                        v.tensor_tensor(dst[:], dst[:],
                                        shset[:, i * 256:(i + 1) * 256], op)
                v.drain()
                v.tensor_scalar_mul(gmin[:], gmin[:], -1.0)
                v.drain()
                return v.tensor_tensor(gmax[:], gmax[:], gmin[:], ALU.max)
            gv = b.stepV(gfold)

            def tpg(t, gv=gv, sprev=b.cnt["S"]):
                b.w(t, ("V", gv), ("S", sprev))
                t.transpose(pS1[:, 0:128], gmax[:, 0:128], identf[:])
                return t.transpose(pS1[:, 128:256], gmax[:, 128:256], identf[:])
            tv = b.stepT(tpg)

            def mxred(v, tv=tv):
                b.w(v, ("T", tv))
                v.tensor_reduce(mxt[:, 0:1], pS1[:, 0:128], axis=AX, op=ALU.max)
                v.drain()
                v.tensor_reduce(mxt[:, 1:2], pS1[:, 128:256], axis=AX, op=ALU.max)
                v.drain()
                v.tensor_scalar_max(mxt[:, 0:2], mxt[:, 0:2], 1e-5)
                v.drain()
                v.tensor_scalar(r2t[:, 0:2], mxt[:, 0:2], m2d[:], None, ALU.mult)
                v.reciprocal(s2t[:, 0:2], mxt[:, 0:2])
                v.drain()
                return v.tensor_scalar_mul(s2t[:, 0:2], s2t[:, 0:2], 127.0)
            vv = b.stepV(mxred)

            def tps(t, vv=vv):
                b.w(t, ("V", vv))
                return t.transpose(pS1[0:2, 256:384], s2t[:, 0:2], identf[:])
            tv2 = b.stepT(tps)

            def evs(s, tv2=tv2):
                b.w(s, ("T", tv2))
                return s.copy(s2r[:], pS1[0:2, 256:384])
            sv = b.stepS(evs)
            for r in range(2):
                sdv = b.dma("gpsimd", "dg", s2pad[64:65, r * 128:(r + 1) * 128],
                            s2r[r:r + 1, :], waits=([("S", sv)] if r == 0 else []))

            def bs2m(t, sdv=sdv, vprev=b.cnt["V"]):
                b.w(t, ("dg", sdv), ("V", vprev))
                return t.matmul(pS1[:, 256:512], ones128[:], s2pad[:],
                                start=True, stop=True)
            tvb = b.stepT(bs2m)

            def bs2e(v, tvb=tvb):
                b.w(v, ("T", tvb))
                return v.tensor_copy(Bs2[:], pS1[:, 256:512])
            vvb = b.stepV(bs2e)

            def qmul(v, vvb=vvb):
                b.w(v, ("V", vvb))
                v.drain()
                return v.tensor_tensor(
                    shset[:].rearrange("p (c f) -> p c f", c=NF),
                    shset[:].rearrange("p (c f) -> p c f", c=NF),
                    Bs2[:, None, :].broadcast_to([128, NF, TG])
                    if False else Bs2[:][:, None, :].broadcast_to([128, NF, TG]),
                    ALU.mult)
            vvq = b.stepV(qmul)

            def qrnd(v, vvq=vvq):
                b.w(v, ("V", vvq))
                v.tensor_scalar(shset[:], shset[:], cmagic[:], None, ALU.add)
                return v.tensor_scalar(sqset[:], shset[:], cmagic[:], None,
                                       ALU.subtract)
            gq = b.stepV(qrnd)

            for dffc in range(NF):
                half = dffc % 2
                g2 = len(g2hist)
                w2dv = b.dma("gpsimd", "dg", wq2b[:, half * 1024:(half + 1) * 1024],
                             wq2t[dffc],
                             waits=([("T", g2hist[g2 - 2])] if g2 >= 2
                                    else [("ds", w2_spill_done)]))

                def m2f(t, dffc=dffc, half=half, w2dv=w2dv,
                        gq=gq, vnow=b.cnt["V"]):
                    b.w(t, ("dg", w2dv), ("V", gq))
                    if dffc == 0:
                        b.w(t, ("V", vnow))
                    last = None
                    for tcrel in range(2):
                        pY = pBig[:, tcrel * 1024:(tcrel + 1) * 1024]
                        for s2 in range(2):
                            last = t.matmul(
                                pY[:, s2 * 512:(s2 + 1) * 512],
                                sqset[:, dffc * TG + tcrel * 128:dffc * TG + tcrel * 128 + 128],
                                wq2b[:, half * 1024 + s2 * 512:half * 1024 + (s2 + 1) * 512],
                                start=(dffc == 0), stop=(dffc == NF - 1))
                    return last
                g2hist.append(b.stepT(m2f))
            tvy = b.cnt["T"]
            for tcrel in range(2):
                tc = tg * 2 + tcrel
                pY = pBig[:, tcrel * 1024:(tcrel + 1) * 1024]

                def yev(v, tc=tc, tcrel=tcrel, pY=pY, tvy=tvy):
                    b.w(v, ("T", tvy))
                    xs = xbuf_c[:, tc, :]
                    return v.scalar_tensor_tensor(xs, pY[:], r2t[:, tcrel:tcrel + 1],
                                                  xs, ALU.mult, ALU.add)
                b.stepV(yev)

        if DBG:
            for nm, tt in (("dbg_mtv", mtv), ("dbg_dqt", dqt), ("dbg_shset", shset),
                           ("dbg_sqset", sqset), ("dbg_r2t", r2t), ("dbg_s2t", s2t),
                           ("dbg_Br", Br), ("dbg_Bs2", Bs2), ("dbg_gmax", gmax), ("dbg_m", dbgm)):
                ap = nc.dram_tensor(nm, [tt.shape[0], tt.shape[1]], tt.dtype,
                                    kind="ExternalOutput").ap()
                b.dma("sync", "do", ap, tt[:],
                      waits=[("V", b.cnt["V"]), ("S", b.cnt["S"]), ("T", b.cnt["T"])])
        stream_ln(xbuf)


def emit_l1_attn(b, w, ck, env):
    sem = b.sem
    nc = b.nc
    xbuf, xbuf_c = env["xbuf"], env["xbuf_c"]
    fmA, wsb, btile, xcb = env["fmA"], env["wsb"], env["btile"], env["xcb"]
    ident, identf = env["ident"], env["identf"]
    st1, st2 = env["st1"], env["st2"]
    qch = env["qch"]
    pT, pS1, pW, pBig = env["pT"], env["pS1"], env["pW"], env["pBig"]
    cEPSL = env["cEPSL"]
    stream_ln, mm_acc = env["stream_ln"], env["mm_acc"]

    w1blk = nc.dram_tensor("w1blk_scr", [32, 128, 1024], BF16, kind="Internal").ap()
    w2blk = nc.dram_tensor("w2blk_scr", [32, 128, 1024], BF16, kind="Internal").ap()
    kfscr = nc.dram_tensor("kf_scr", [NT, 128, 1024], BF16, kind="Internal").ap()

    with ExitStack() as sc:
        b.autoflush = ("head" in AFR) or ("all" in AFR)
        scB = ExitStack()
        scC = ExitStack()
        zch = b.sbuf("zch", [128, D], F32, sc)
        e1 = env["qsq"]
        e2 = env["qch"]
        G1t = b.sbuf("G1t", [128, D], BF16, sc)
        BE1t = b.sbuf("BE1t", [128, D], BF16, sc)
        ebf = b.sbuf("ebf", [128, 2 * 512], BF16, sc)
        wcb = b.sbuf("wcb", [128, 2 * 1024], BF16, sc)
        gdv = b.dma("gpsimd", "dg", G1t[:],
                    w["L1_attn_g1"][None, :].broadcast_to([128, D]))
        bdv = b.dma("gpsimd", "dg", BE1t[:],
                    w["L1_attn_be1"][None, :].broadcast_to([128, D]))

        def ln_gb_transpose(src_c, gt, bt, gbdv):
            for tc in range(NT):
                ch = src_c[:, tc, :]

                def stats(v, ch=ch, tc=tc):
                    if tc == 0:
                        b.w(v, ("dg", gbdv))
                    v.bn_stats(st1[:, 0:6], ch[:, 0:512])
                    v.bn_stats(st1[:, 6:12], ch[:, 512:1024])
                    v.drain()
                    return v.bn_aggr(st2[:, 0:2], st1[:, 0:12])
                vv = b.stepV(stats)

                def sq(s, vv=vv):
                    b.w(s, ("V", vv))
                    return s.activation(st2[:, 2:3], st2[:, 1:2], AF.Sqrt,
                                        bias=cEPSL[:])
                sv = b.stepS(sq)

                def appl(v, ch=ch, sv=sv, tprev=b.cnt["T"]):
                    b.w(v, ("S", sv), ("T", tprev))
                    v.reciprocal(st2[:, 3:4], st2[:, 2:3])
                    v.drain()
                    v.tensor_scalar(st2[:, 4:5], st2[:, 0:1], st2[:, 3:4], -1.0,
                                    ALU.mult, ALU.mult)
                    v.drain()
                    v.tensor_scalar(zch[:], ch, st2[:, 3:4], st2[:, 4:5],
                                    ALU.mult, ALU.add)
                    v.tensor_tensor(zch[:], zch[:], gt[:], ALU.mult)
                    return v.tensor_tensor(xcb[:], zch[:], bt[:], ALU.add)
                vv2 = b.stepV(appl)

                def tp(t, vv2=vv2, sprev=b.cnt["S"]):
                    b.w(t, ("V", vv2), ("S", sprev))
                    for dc in range(7):
                        t.transpose(pT[:, dc * 128:(dc + 1) * 128],
                                    xcb[:, dc * 128:(dc + 1) * 128], ident[:])
                    return t.transpose(pT[:, 7 * 128:8 * 128],
                                       xcb[:, 7 * 128:8 * 128], ident[:])
                tv = b.stepT(tp)

                def evac(s, tc=tc, tv=tv):
                    b.w(s, ("T", tv))
                    for dc in range(7):
                        s.copy(fmA[:, dc * T + tc * 128:dc * T + (tc + 1) * 128],
                               pT[:, dc * 128:(dc + 1) * 128])
                    return s.copy(fmA[:, 7 * T + tc * 128:7 * T + (tc + 1) * 128],
                                  pT[:, 7 * 128:8 * 128])
                b.stepS(evac)

        ln_gb_transpose(xbuf_c, G1t, BE1t, b.cnt["dg"])
        ck("s3ln1")
        b.autoflush = ("qkvr" in AFR) or ("all" in AFR)

        qfT = b.sbuf("qfT", [128, NT * D], BF16, scC)
        kvb = b.sbuf("kvb", [128, 16 * 65], BF16, scC)
        kfc = b.sbuf("kfc", [128, 2 * 1024], BF16, scB)
        vA = b.sbuf("vA", [128, NT * 1040], BF16, scB)
        vA_c = vA[:].rearrange("p (c h f) -> p c h f", c=NT, h=16)

        def vmisc(v):
            return v.memset(vA_c[:, :, :, 64:65], 1.0)
        v_va = b.stepV(vmisc)

        qkv_w_ap = w["L1_attn_qkv_w"].rearrange("(c p) f -> p c f", p=128)
        af_qkv = b.autoflush
        for strip in range(6):
            if "strip" in AFR:
                b.autoflush = af_qkv or (("st%d" % strip) in AFR)
            c0 = strip * 512
            wdv = b.dma("gpsimd", "dg",
                        wsb[:, 0:ND * 512].rearrange("p (c f) -> p c f", c=ND),
                        qkv_w_ap[:, :, c0:c0 + 512], waits=[("T", b.cnt["T"])])
            bdv2 = b.dma("gpsimd", "dg", btile[:, 0:512],
                         w["L1_attn_qkv_b"][None, c0:c0 + 512].broadcast_to([128, 512]),
                         waits=[("V", b.cnt["V"])])
            for tc in range(NT):
                po = pS1[:, 0:512] if tc % 2 == 0 else pW[:, 0:512]
                prev_v = b.cnt["V"]
                prev_s = b.cnt["S"]

                def qmm(t, tc=tc, po=po, wdv=wdv, prev_v=prev_v, prev_s=prev_s):
                    b.w(t, ("dg", wdv), ("S", prev_s), ("V", prev_v))
                    return mm_acc(t, po,
                                  lambda dc, tc=tc: fmA[:, dc * T + tc * 128:dc * T + (tc + 1) * 128],
                                  lambda dc: wsb[:, dc * 512:(dc + 1) * 512], ND)
                tv = b.stepT(qmm)

                if strip < 4:
                    dc0 = (strip % 2) * 512
                    is_q = strip < 2

                    def elu_a(v, tv=tv, po=po):
                        b.w(v, ("T", tv), ("dg", bdv2))
                        v.tensor_tensor(e1[:], po, btile[:, 0:512], ALU.add)
                        v.drain()
                        return v.tensor_scalar(e2[:], e1[:], 0.0, None, ALU.min)
                    vv1 = b.stepV(elu_a)

                    def elu_b(s, vv1=vv1):
                        b.w(s, ("V", vv1))
                        return s.activation(e2[:], e2[:], AF.Exp)
                    sv1 = b.stepS(elu_b)

                    if is_q:
                        def elu_c(v, tc=tc, dc0=dc0, sv1=sv1):
                            b.w(s if False else v, ("S", sv1))
                            v.tensor_relu(e1[:], e1[:])
                            v.drain()
                            return v.tensor_tensor(
                                qfT[:, tc * D + dc0:tc * D + dc0 + 512],
                                e1[:], e2[:], ALU.add)
                        b.stepV(elu_c)
                    else:
                        eb = ebf[:, (tc % 2) * 512:(tc % 2) * 512 + 512]

                        def elu_ck(v, sv1=sv1, eb=eb, dprev=b.cnt["ds"]):
                            b.w(v, ("S", sv1), ("ds", dprev))
                            v.tensor_relu(e1[:], e1[:])
                            v.drain()
                            return v.tensor_tensor(eb, e1[:], e2[:], ALU.add)
                        vv3 = b.stepV(elu_ck)
                        b.dma("sync", "ds", kfscr[tc][:, dc0:dc0 + 512], eb,
                              waits=[("V", vv3)])
                else:
                    h0 = (strip - 4) * 4

                    def vev(v, tc=tc, tv=tv, po=po, strip=strip):
                        b.w(v, ("T", tv), ("dg", bdv2))
                        return v.scalar_tensor_tensor(
                            vA_c[:, tc, (strip - 4) * 8:(strip - 4) * 8 + 8, 0:64],
                            po.rearrange("p (h f) -> p h f", h=8),
                            1.0,
                            btile[:, 0:512].rearrange("p (h f) -> p h f", h=8),
                            ALU.mult, ALU.add)
                    b.stepV(vev)
        ck("s3qkv")
        b.autoflush = ("kvr" in AFR) or ("all" in AFR)

        w1ap = w["L1_attn_w1"].rearrange("(c p) f -> p c f", p=128)
        w2ap = w["L1_attn_w2"].rearrange("(c p) f -> p c f", p=128)
        _pp = []
        for i in range(32):
            dc, piece = i // 4, i % 4
            half = i % 2
            dv = b.dma("gpsimd", "dg", wcb[:, half * 1024:(half + 1) * 1024],
                       w1ap[:, dc, piece * 1024:(piece + 1) * 1024],
                       waits=([("ds", _pp[-2])] if len(_pp) >= 2 else []))
            _pp.append(b.dma("sync", "ds",
                             w1blk.rearrange("c p f -> p c f", c=32)
                             [:, piece * 8:piece * 8 + 8, dc * 128:(dc + 1) * 128],
                             wcb[:, half * 1024:(half + 1) * 1024]
                             .rearrange("p (c f) -> p c f", c=8),
                             waits=[("dg", dv)]))
        for fc in range(32):
            half = fc % 2
            dv = b.dma("gpsimd", "dg", wcb[:, half * 1024:(half + 1) * 1024],
                       w2ap[:, fc, :],
                       waits=([("ds", _pp[-2])] if len(_pp) >= 2 else []))
            _pp.append(b.dma("sync", "ds", w2blk[fc],
                             wcb[:, half * 1024:(half + 1) * 1024],
                             waits=[("dg", dv)]))
        _spill_done_s3 = b.cnt["ds"]

        def kvz(v):
            return v.memset(kvb[:], 0.0)
        v_kvz = b.stepV(kvz)

        for kc in range(NT):
            half = kc % 2
            kdv = b.dma("sync", "ds", kfc[:, half * 1024:(half + 1) * 1024],
                        kfscr[kc], waits=[("T", b.cnt["T"])] if kc >= 2 else [])

            def kvmm(t, kc=kc, half=half, kdv=kdv, vprev=b.cnt["V"]):
                b.w(t, ("ds", kdv))
                if kc == 0:
                    b.w(t, ("V", vprev))
                last = None
                for h in range(H):
                    st_flag = (kc == 0 and h % 4 == 0)
                    last = t.matmul(pBig[0:64, h * 128:h * 128 + 65],
                                    kfc[:, half * 1024 + h * 64:half * 1024 + (h + 1) * 64],
                                    vA[:, kc * 1040 + h * 65:kc * 1040 + h * 65 + 65],
                                    start=st_flag,
                                    stop=(kc == NT - 1))
                return last
            b.stepT(kvmm)
        tv_kv = b.cnt["T"]

        def kv_ev(v, tv_kv=tv_kv):
            b.w(v, ("T", tv_kv))
            return v.tensor_copy(
                kvb[0:64, :].rearrange("p (h f) -> p h f", h=16),
                pBig[0:64, :].rearrange("p (h f) -> p h f", h=16)[:, :, 0:65])
        v_kv = b.stepV(kv_ev)
        b.flush()
        scB.close()
        ck("s3kv")
        b.autoflush = ("yreg" in AFR) or ("all" in AFR)

        yTM = b.sbuf("yTM", [128, NT * D], BF16, scC)
        qfA = b.sbuf("qfA", [128, T], BF16, scC)

        def qfz(v):
            return v.memset(qfA[:], 0.0)
        b.stepV(qfz)

        for h in range(H):
            for half in range(2):
                def tpq(t, h=h, half=half, sprev=b.cnt["S"], vprev=b.cnt["V"]):
                    b.w(t, ("S", sprev), ("V", vprev))
                    last = None
                    for j in range(8):
                        tc = half * 8 + j
                        last = t.transpose(pT[0:64, j * 128:(j + 1) * 128],
                                           qfT[:, tc * D + h * 64:tc * D + (h + 1) * 64],
                                           ident[:])
                    return last
                tvq = b.stepT(tpq)

                def evq(s, half=half, tvq=tvq):
                    b.w(s, ("T", tvq))
                    return s.copy(qfA[0:64, half * 1024:(half + 1) * 1024],
                                  pT[0:64, 0:1024])
                b.stepS(evq)
            sv_q = b.cnt["S"]
            for tc in range(NT):
                po = pS1[:, 0:128] if tc % 2 == 0 else pW[:, 0:128]

                def ymm(t, h=h, tc=tc, po=po, sv_q=sv_q, v_kv=v_kv,
                        vprev=b.cnt["V"]):
                    b.w(t, ("S", sv_q), ("V", max(v_kv, vprev)))
                    return t.matmul(po[:, 0:65], qfA[:, tc * 128:(tc + 1) * 128],
                                    kvb[:, h * 65:(h + 1) * 65],
                                    start=True, stop=True)
                tvy = b.stepT(ymm)

                def yev(v, h=h, tc=tc, po=po, tvy=tvy):
                    b.w(v, ("T", tvy))
                    v.tensor_scalar_max(st1[:, 0:1], po[:, 64:65], 1e-6)
                    v.drain()
                    v.reciprocal(st1[:, 1:2], st1[:, 0:1])
                    v.drain()
                    return v.tensor_scalar(yTM[:, tc * D + h * 64:tc * D + (h + 1) * 64],
                                           po[:, 0:64], st1[:, 1:2], None, ALU.mult)
                b.stepV(yev)
        ck("s3y")

        def y_transpose():
            for tc in range(NT):
                def tp(t, tc=tc, sprev=b.cnt["S"], vprev=b.cnt["V"]):
                    b.w(t, ("S", sprev), ("V", vprev))
                    last = None
                    for dc in range(ND):
                        last = t.transpose(pT[:, dc * 128:(dc + 1) * 128],
                                           yTM[:, tc * D + dc * 128:tc * D + (dc + 1) * 128],
                                           ident[:])
                    return last
                tv = b.stepT(tp)

                def evac(s, tc=tc, tv=tv):
                    b.w(s, ("T", tv))
                    last = None
                    for dc in range(ND):
                        last = s.copy(fmA[:, dc * T + tc * 128:dc * T + (tc + 1) * 128],
                                      pT[:, dc * 128:(dc + 1) * 128])
                    return last
                b.stepS(evac)
        y_transpose()
        b.barrier()
        scC.close()
        b.autoflush = ("outp" in AFR) or ("all" in AFR)
        xx = b.sbuf("xx", [128, NT * D], F32, sc)
        xx_c = xx[:].rearrange("p (c f) -> p c f", c=NT)

        ow_ap = w["L1_attn_out_w"].rearrange("(c p) f -> p c f", p=128)
        for strip in range(2):
            c0 = strip * 512
            wdv = b.dma("gpsimd", "dg",
                        wsb[:, 0:ND * 512].rearrange("p (c f) -> p c f", c=ND),
                        ow_ap[:, :, c0:c0 + 512], waits=[("T", b.cnt["T"])])
            bdv3 = b.dma("gpsimd", "dg", btile[:, 0:512],
                         w["L1_attn_out_b"][None, c0:c0 + 512].broadcast_to([128, 512]),
                         waits=[("V", b.cnt["V"])])
            for tc in range(NT):
                po = pS1[:, 0:512] if tc % 2 == 0 else pW[:, 0:512]
                prev_v = b.cnt["V"]

                def omm(t, tc=tc, po=po, wdv=wdv, prev_v=prev_v, sprev=b.cnt["S"]):
                    b.w(t, ("dg", wdv), ("S", sprev), ("V", prev_v))
                    return mm_acc(t, po,
                                  lambda dc, tc=tc: fmA[:, dc * T + tc * 128:dc * T + (tc + 1) * 128],
                                  lambda dc: wsb[:, dc * 512:(dc + 1) * 512], ND)
                tv = b.stepT(omm)

                def oev(v, tc=tc, tv=tv, po=po, c0=c0):
                    b.w(v, ("T", tv), ("dg", bdv3))
                    v.tensor_tensor(qch[:], po, btile[:, 0:512], ALU.add)
                    v.drain()
                    return v.tensor_tensor(xx_c[:, tc, c0:c0 + 512],
                                           xbuf_c[:, tc, c0:c0 + 512],
                                           qch[:], ALU.add)
                b.stepV(oev)
        ck("s3out")
        b.autoflush = b.autoflush or ("ln2" in AFR)

        gdv2 = b.dma("gpsimd", "dg", G1t[:],
                     w["L1_attn_g2"][None, :].broadcast_to([128, D]),
                     waits=[("V", b.cnt["V"])])
        bdv4 = b.dma("gpsimd", "dg", BE1t[:],
                     w["L1_attn_be2"][None, :].broadcast_to([128, D]),
                     waits=[("V", b.cnt["V"])])
        ln_gb_transpose(xx_c, G1t, BE1t, b.cnt["dg"])
        ck("s3ln2")
        b.autoflush = ("mlp" in AFR) or ("all" in AFR)

        b1t = b.sbuf("b1t", [128, 32], F32, sc)
        B2t = b.sbuf("B2t", [128, D], F32, sc)
        hmb = b.sbuf("hmb", [128, 2 * 256], BF16, sc)
        w1b = wsb
        w2b = b.sbuf("w2b", [128, 4 * 1024], BF16, sc)
        b1dv = b.dma("sync", "ds", b1t[:],
                     w["L1_attn_b1"].rearrange("(c p) -> p c", p=128), slow=True)
        b2dv = b.dma("gpsimd", "dg", B2t[:],
                     w["L1_attn_b2"][None, :].broadcast_to([128, D]))
        h1_mlp, h2_mlp = [], []
        mlp_spill_done = _spill_done_s3
        for tg in range(8):
            t0 = tg * 256
            for fc in range(32):
                half = fc % 2
                g = len(h1_mlp)
                qtr = g % 4
                pm = pS1[:, 0:256] if half == 0 else pW[:, 0:256]
                hv = hmb[:, half * 256:(half + 1) * 256]
                w1dv = b.dma("sync", "ds", w1b[:, qtr * 1024:(qtr + 1) * 1024],
                             w1blk[fc],
                             waits=[("T", h1_mlp[g - 4])] if g >= 4
                             else [("T", b.cnt["T"])])
                w2dv = b.dma("gpsimd", "dg", w2b[:, qtr * 1024:(qtr + 1) * 1024],
                             w2blk[fc],
                             waits=[("T", h2_mlp[g - 4])] if g >= 4
                             else [("ds", mlp_spill_done)])

                def m1f(t, fc=fc, qtr=qtr, pm=pm, w1dv=w1dv, t0=t0,
                        vprev=b.cnt["V"], sprev=b.cnt["S"]):
                    b.w(t, ("ds", w1dv), ("V", vprev), ("S", sprev))
                    return mm_acc(t, pm,
                                  lambda dc, qtr=qtr: w1b[:, qtr * 1024 + dc * 128:qtr * 1024 + (dc + 1) * 128],
                                  lambda dc, t0=t0: fmA[:, dc * T + t0:dc * T + t0 + 256],
                                  ND)
                tv1 = b.stepT(m1f)
                h1_mlp.append(tv1)

                def gel(s, fc=fc, tv1=tv1, pm=pm, hv=hv, b1dv=b1dv):
                    b.w(s, ("T", tv1), ("ds", b1dv))
                    return s.activation(hv, pm, AF.Gelu, bias=b1t[:, fc:fc + 1])
                sv1 = b.stepS(gel)

                def m2f(t, fc=fc, qtr=qtr, tcref=tg, sv1=sv1, w2dv=w2dv, hv=hv):
                    b.w(t, ("S", sv1), ("dg", w2dv))
                    last = None
                    for tcrel in range(2):
                        pY = pBig[:, tcrel * 1024:(tcrel + 1) * 1024]
                        for s2 in range(2):
                            last = t.matmul(
                                pY[:, s2 * 512:(s2 + 1) * 512],
                                hv[:, tcrel * 128:(tcrel + 1) * 128],
                                w2b[:, qtr * 1024 + s2 * 512:qtr * 1024 + (s2 + 1) * 512],
                                start=(fc == 0), stop=(fc == 31))
                    return last
                h2_mlp.append(b.stepT(m2f))
            tvy2 = b.cnt["T"]

            for tcrel in range(2):
                tc = tg * 2 + tcrel
                pY = pBig[:, tcrel * 1024:(tcrel + 1) * 1024]

                def fev(v, tc=tc, pY=pY, tvy2=tvy2, b2dv=b2dv):
                    b.w(v, ("T", tvy2), ("dg", b2dv))
                    v.tensor_tensor(zch[:], pY, B2t[:], ALU.add)
                    v.drain()
                    v.tensor_tensor(zch[:], zch[:], xx_c[:, tc, :], ALU.add)
                    v.drain()
                    return v.tensor_tensor(xbuf_c[:, tc, :], xbuf_c[:, tc, :],
                                           zch[:], ALU.add)
                b.stepV(fev)
        b.autoflush = ("sln" in AFR) or ("all" in AFR)
        stream_ln(xbuf)
        b.flush()
        b.autoflush = "all" in AFR


def emit_l1_ffn(b, w, ck, env):
    sem = b.sem
    nc = b.nc
    xbuf, xbuf_c = env["xbuf"], env["xbuf_c"]
    fmA, xcb = env["fmA"], env["xcb"]
    ident = env["ident"]
    pT, pS1, pW, pBig = env["pT"], env["pS1"], env["pW"], env["pBig"]
    stream_ln, mm_acc = env["stream_ln"], env["mm_acc"]
    cast_transpose = env["cast_transpose"]
    S127 = 1.0 / 127.0
    NFC = 24

    wgt = nc.dram_tensor("wgt_scr", [NFC, 128, 1024], BF16, kind="Internal").ap()
    wut = nc.dram_tensor("wut_scr", [NFC, 128, 1024], BF16, kind="Internal").ap()
    wdt = nc.dram_tensor("wdt_scr", [NFC, 128, 1024], BF16, kind="Internal").ap()

    with ExitStack() as sc:
        wcb = b.sbuf("wcb4", [128, 2 * 1024], BF16, sc)
        ttile = b.sbuf("ttile4", [128, 2 * 1024], BF16, sc)
        sgb = b.sbuf("sgb", [128, 256], F32, sc)
        hmb = b.sbuf("hmb4", [128, 2 * 256], BF16, sc)
        wgb = b.sbuf("wgb", [128, 4 * 1024], BF16, sc)
        wub = b.sbuf("wub", [128, 4 * 1024], BF16, sc)
        wdb = b.sbuf("wdb", [128, 4 * 1024], BF16, sc)

        cast_transpose(lambda tc: xbuf_c[:, tc, :], fmA, ND,
                       waits=(("V", b.cnt["V"]),))

        def prep_t(src_ap_of, n, dst_of, st):
            t_hist, s_hist, d_hist = st["t"], st["s"], st["d"]
            for i in range(n):
                g = len(t_hist)
                half = g % 2
                dv = b.dma("gpsimd", "dg", wcb[:, half * 1024:(half + 1) * 1024],
                           src_ap_of(i),
                           waits=[("T", t_hist[g - 2])] if g >= 2
                           else [("S", b.cnt["S"])])

                def tpp(t, dv=dv, half=half,
                        sprev=(s_hist[g - 1] if g >= 1 else b.cnt["S"])):
                    b.w(t, ("dg", dv), ("S", sprev))
                    last = None
                    for dc in range(8):
                        last = t.transpose(pT[:, dc * 128:(dc + 1) * 128],
                                           wcb[:, half * 1024 + dc * 128:half * 1024 + (dc + 1) * 128],
                                           ident[:])
                    return last
                tv = b.stepT(tpp)
                t_hist.append(tv)

                def evt(s, tv=tv, half=half,
                        dprev=(d_hist[g - 2] if g >= 2 else b.cnt["ds"])):
                    b.w(s, ("T", tv), ("ds", dprev))
                    for dc in range(7):
                        s.copy(ttile[:, half * 1024 + dc * 128:half * 1024 + (dc + 1) * 128],
                               pT[:, dc * 128:(dc + 1) * 128])
                    return s.copy(ttile[:, half * 1024 + 7 * 128:half * 1024 + 8 * 128],
                                  pT[:, 7 * 128:8 * 128])
                sv = b.stepS(evt)
                s_hist.append(sv)
                d_hist.append(b.dma("sync", "ds", dst_of(i),
                                    ttile[:, half * 1024:(half + 1) * 1024],
                                    waits=[("S", sv)]))

        wg_ap = w["L1_ffn_w_gate"].rearrange("(c p) f -> p c f", p=128)
        wu_ap = w["L1_ffn_w_up"].rearrange("(c p) f -> p c f", p=128)
        wd_ap = w["L1_ffn_w_down"].rearrange("(c p) f -> p c f", p=128)
        _pst = {"t": [], "s": [], "d": []}
        prep_t(lambda i: wg_ap[:, i, :], NFC, lambda i: wgt[i], _pst)
        prep_t(lambda i: wu_ap[:, i, :], NFC, lambda i: wut[i], _pst)
        prep_t(lambda i: wd_ap[:, i // 3, (i % 3) * 1024:(i % 3 + 1) * 1024], NFC,
               lambda i: wdt.rearrange("c p f -> p c f", c=NFC)
               [:, (i % 3) * 8:(i % 3) * 8 + 8, (i // 3) * 128:(i // 3 + 1) * 128],
               _pst)
        ck("s4prep")

        gu_hist, d_hist2 = [], []
        prep_done = b.cnt["ds"]
        for tg in range(8):
            t0 = tg * 256
            for fc in range(NFC):
                g = len(gu_hist)
                qtr = g % 4
                half = fc % 2
                gdv = b.dma("sync", "ds", wgb[:, qtr * 1024:(qtr + 1) * 1024],
                            wgt[fc],
                            waits=[("T", gu_hist[g - 4])] if g >= 4
                            else [("ds", prep_done)])
                udv = b.dma("gpsimd", "dg", wub[:, qtr * 1024:(qtr + 1) * 1024],
                            wut[fc],
                            waits=[("T", gu_hist[g - 4])] if g >= 4
                            else [("ds", prep_done)])
                ddv = b.dma("sync", "ds", wdb[:, qtr * 1024:(qtr + 1) * 1024],
                            wdt[fc],
                            waits=[("T", d_hist2[g - 4])] if g >= 4 else [])

                def gumm(t, fc=fc, qtr=qtr, t0=t0, gdv=gdv, udv=udv,
                         sprev=b.cnt["S"], vprev=b.cnt["V"]):
                    b.w(t, ("ds", gdv), ("dg", udv), ("S", sprev), ("V", vprev))
                    mm_acc(t, pS1[:, 0:256],
                           lambda dc, qtr=qtr: wgb[:, qtr * 1024 + dc * 128:qtr * 1024 + (dc + 1) * 128],
                           lambda dc, t0=t0: fmA[:, dc * T + t0:dc * T + t0 + 256],
                           ND)
                    return mm_acc(t, pW[:, 0:256],
                                  lambda dc, qtr=qtr: wub[:, qtr * 1024 + dc * 128:qtr * 1024 + (dc + 1) * 128],
                                  lambda dc, t0=t0: fmA[:, dc * T + t0:dc * T + t0 + 256],
                                  ND)
                tv1 = b.stepT(gumm)
                gu_hist.append(tv1)

                def sil(s, tv1=tv1):
                    b.w(s, ("T", tv1))
                    return s.activation(sgb[:], pS1[:, 0:256], AF.Silu, scale=S127)
                sv1 = b.stepS(sil)

                def hev(v, half=half, sv1=sv1):
                    b.w(v, ("S", sv1))
                    return v.scalar_tensor_tensor(
                        hmb[:, half * 256:(half + 1) * 256],
                        pW[:, 0:256], S127, sgb[:], ALU.mult, ALU.mult)
                vv1 = b.stepV(hev)

                def dmm(t, fc=fc, half=half, qtr=qtr, vv1=vv1, ddv=ddv):
                    b.w(t, ("V", vv1), ("ds", ddv))
                    last = None
                    for tcrel in range(2):
                        pY = pBig[:, tcrel * 1024:(tcrel + 1) * 1024]
                        for s2 in range(2):
                            last = t.matmul(
                                pY[:, s2 * 512:(s2 + 1) * 512],
                                hmb[:, half * 256 + tcrel * 128:half * 256 + tcrel * 128 + 128],
                                wdb[:, qtr * 1024 + s2 * 512:qtr * 1024 + (s2 + 1) * 512],
                                start=(fc == 0), stop=(fc == NFC - 1))
                    return last
                d_hist2.append(b.stepT(dmm))
            tvy = b.cnt["T"]

            for tcrel in range(2):
                tc = tg * 2 + tcrel
                pY = pBig[:, tcrel * 1024:(tcrel + 1) * 1024]

                def fev(v, tc=tc, pY=pY, tvy=tvy):
                    b.w(v, ("T", tvy))
                    return v.scalar_tensor_tensor(xbuf_c[:, tc, :], pY, S127,
                                                  xbuf_c[:, tc, :],
                                                  ALU.mult, ALU.add)
                b.stepV(fev)
        stream_ln(xbuf)
        b.flush()


from concourse.bass_utils import run_bass_kernel_spmd

_WSHAPES = {
    "L0_attn_Wq": ((1024, 1040), F32), "L0_attn_bq": ((1040,), F32),
    "L0_attn_Wk": ((1024, 1024), F32), "L0_attn_bk": ((1024,), F32),
    "L0_attn_Wv": ((1024, 1024), F32), "L0_attn_bv": ((1024,), F32),
    "L0_attn_Wo": ((1024, 1024), F32), "L0_attn_bo": ((1024,), F32),
    "L0_attn_qn_w": ((64,), F32), "L0_attn_kn_w": ((64,), F32),
    "L0_ffn_w1": ((4096, 1024), F32), "L0_ffn_conv_w": ((4096, 1, 3), F32),
    "L0_ffn_conv_b": ((4096,), F32), "L0_ffn_w2": ((1024, 4096), F32),
    "L1_attn_qkv_w": ((1024, 3072), F32), "L1_attn_qkv_b": ((3072,), F32),
    "L1_attn_out_w": ((1024, 1024), F32), "L1_attn_out_b": ((1024,), F32),
    "L1_attn_w1": ((1024, 4096), F32), "L1_attn_b1": ((4096,), F32),
    "L1_attn_w2": ((4096, 1024), F32), "L1_attn_b2": ((1024,), F32),
    "L1_attn_g1": ((1024,), F32), "L1_attn_be1": ((1024,), F32),
    "L1_attn_g2": ((1024,), F32), "L1_attn_be2": ((1024,), F32),
    "L1_ffn_w_gate": ((3072, 1024), I32), "L1_ffn_w_up": ((3072, 1024), I32),
    "L1_ffn_w_down": ((1024, 3072), I32),
}


def _build_nc():
    nc = bass.Bass("TRN2", target_bir_lowering=False)
    x = nc.dram_tensor("x", [2048, 1024], F32, kind="ExternalInput").ap()
    w = {}
    for k, (shp, dt) in _WSHAPES.items():
        w[k] = nc.dram_tensor(k, list(shp), dt, kind="ExternalInput").ap()
    w["cident"] = nc.dram_tensor("cident", [128, 128], BF16,
                                 kind="ExternalInput").ap()
    out = nc.dram_tensor("out", [2048, 1024], F32, kind="ExternalOutput").ap()
    build(nc, x, w, out, n_stages=4)
    return nc


def kernel(**inputs):
    xs = np.ascontiguousarray(np.asarray(inputs["L0_attn_x"], dtype=np.float32))
    nc = _build_nc()
    base = {k: np.ascontiguousarray(np.asarray(inputs[k])) for k in _WSHAPES}
    base["cident"] = np.eye(128, dtype=ml_dtypes.bfloat16)
    in_maps = [dict(base, x=xs[i]) for i in range(8)]
    res = run_bass_kernel_spmd(nc, in_maps, core_ids=list(range(8)))
    return np.stack([np.asarray(r["out"], dtype=np.float32)
                     for r in res.results], axis=0)


# version 09 (final)
# speedup vs baseline: 2.0732x; 1.1264x over previous
import os
import numpy as np
import ml_dtypes
from contextlib import ExitStack
import concourse.bass as bass
import concourse.mybir as mybir

F32 = mybir.dt.float32
BF16 = mybir.dt.bfloat16
I32 = mybir.dt.int32
AF = mybir.ActivationFunctionType
ALU = mybir.AluOpType
AX = mybir.AxisListType.X

NT, D, ND, H, HD, T = 16, 1024, 8, 16, 64, 2048
MAGIC = 12582912.0
AFR = os.environ.get("KB_AFR", "")


class B:
    def __init__(self, nc, ctx, blk):
        self.nc, self.ctx, self.blk = nc, ctx, blk
        self.q = {"tensor": [], "vector": [], "scalar": [], "gpsimd": []}
        self.autoflush = bool(int(os.environ.get("KB_AUTOFLUSH", "0")))
        self.defer_eng = set(os.environ.get(
            "KB_DEFER_ENG", "tensor,scalar,gpsimd").split(","))
        self.cnt = {}
        self.sem = {}
        for name in ("dl", "dg", "ds", "do", "T", "V", "S", "G"):
            self.sem[name] = ctx.enter_context(nc.semaphore(name))
            self.cnt[name] = 0

    def sbuf(self, name, shape, dtype, stack=None):
        return (stack or self.ctx).enter_context(
            self.nc.sbuf_tensor(name, shape, dtype))

    def psum(self, name, shape, dtype):
        return self.ctx.enter_context(self.nc.psum_tensor(name, shape, dtype))

    def _step(self, attr, semname, f):
        semh = self.sem[semname]

        def g(h):
            last = f(h)
            assert last is not None, "step fn must return its last instruction"
            last.then_inc(semh, 1)
        self.q[attr].append(g)
        self.cnt[semname] += 1
        if self.autoflush or attr not in self.defer_eng:
            self.flush_one(attr)
        return self.cnt[semname]

    def flush_one(self, attr):
        ops = self.q[attr]
        if ops:
            if os.environ.get("KB_FLOG"):
                print("flush %s: %d steps" % (attr, len(ops)))
            getattr(self.blk, attr)(
                lambda h, ops=tuple(ops): [f(h) for f in ops])
            self.q[attr] = []

    def flush(self):
        for attr in ("tensor", "vector", "scalar", "gpsimd"):
            self.flush_one(attr)

    def stepT(self, f):
        return self._step("tensor", "T", f)

    def stepV(self, f):
        return self._step("vector", "V", f)

    def stepS(self, f):
        return self._step("scalar", "S", f)

    def stepG(self, f):
        return self._step("gpsimd", "G", f)

    def dma(self, eng, semname, dst, src, waits=(), slow=False):
        sm = self.sem[semname]
        allsem = self.sem

        def g(e):
            for sn, vv in waits:
                e.wait_ge(allsem[sn], vv)
            e.dma_start(dst, src,
                        allow_slow_non_contiguous=slow).then_inc(sm, 16)
        getattr(self.blk, eng)(g)
        self.cnt[semname] += 16
        return self.cnt[semname]

    def settleV(self, waits=()):
        def f(v):
            self.w(v, *waits)
            v.drain()
            return v.memset(self.bscratch[0:1, 0:4], 0.0)
        return self.stepV(f)

    def w(self, h, *pairs):
        for sn, vv in pairs:
            h.wait_ge(self.sem[sn], vv)

    def barrier(self):
        self.flush()
        snap = {k: self.cnt[k] for k in ("T", "V", "S", "G", "dl", "dg", "ds")}
        for attr, semname in (("tensor", "T"), ("vector", "V"), ("scalar", "S"),
                              ("gpsimd", "G")):
            names = [n for n in snap if n != semname]

            def g(h, names=names, semname=semname, attr=attr):
                for n in names:
                    if snap[n]:
                        h.wait_ge(self.sem[n], snap[n])
                if attr == "vector":
                    last = h.memset(self.bscratch[0:1, 0:4], 0.0)
                elif attr == "scalar":
                    last = h.copy(self.bscratch[0:1, 4:8], self.bscratch[0:1, 0:4])
                else:
                    last = h.engine_nop() if hasattr(h, "engine_nop") else h.nop()
                last.then_inc(self.sem[semname], 1)
            getattr(self.blk, attr)(g)
            self.cnt[semname] += 1

        def gs(e):
            for n in ("T", "V", "S", "G"):
                e.wait_ge(self.sem[n], self.cnt[n])
        self.blk.sync(gs)


def build(nc, x_ap, w, out_ap, n_stages=4, stop=None):
    with ExitStack() as ctx, nc.Block() as blk:
        b = B(nc, ctx, blk)
        b.bscratch = b.sbuf("bscr", [1, 8], F32)
        emit(b, x_ap, w, out_ap, n_stages, stop)
    return nc


def emit(b, x_ap, w, out_ap, n_stages, stop=None):
    class _Stop(Exception):
        pass

    def ck(name):
        if stop == name:
            raise _Stop()

    sem = b.sem

    xbuf = b.sbuf("xbuf", [128, NT * D], F32)
    fmA = b.sbuf("fmA", [128, ND * T], BF16)
    wsb = b.sbuf("wsb", [128, ND * 512], BF16)
    btile = b.sbuf("btile", [128, 512], F32)
    xcb = b.sbuf("xcb", [128, D], BF16)
    cmagic = b.sbuf("cmagic", [128, 1], F32)
    cEPSA = b.sbuf("cEPSA", [128, 1], F32)
    cEPSL = b.sbuf("cEPSL", [128, 1], F32)
    cM16 = b.sbuf("cM16", [128, 1], F32)
    ident = b.sbuf("ident", [128, 128], BF16)
    identf = b.sbuf("identf", [128, 128], F32)
    onespad = b.sbuf("onespad", [128, 128], BF16)
    st1 = b.sbuf("st1", [128, 16], F32)
    st2 = b.sbuf("st2", [128, 16], F32)
    st3 = b.sbuf("st3", [128, 16], F32)
    qch = b.sbuf("qch", [128, 512], F32)
    qsq = b.sbuf("qsq", [128, 512], F32)
    qnb = b.sbuf("qnb", [128, 512], BF16)

    pT = b.psum("pT", [128, 1024], BF16)
    pS1 = b.psum("pS1", [128, 512], F32)
    pW = b.psum("pW", [128, 1024], F32)
    pBig = b.psum("pBig", [128, 2048], F32)

    xbuf_c = xbuf[:].rearrange("p (c f) -> p c f", c=NT)

    b.dma("sync", "dl", ident[:], w["cident"])
    b.dma("sync", "dl", xbuf_c, x_ap.rearrange("(c p) f -> p c f", p=128))
    dl_init = b.cnt["dl"]

    def i_v(v):
        b.w(v, ("dl", 16))
        v.tensor_copy(identf[:], ident[:])
        v.memset(cmagic[:], 0.5)
        v.drain()
        v.tensor_scalar_add(cmagic[:], cmagic[:], MAGIC)
        v.drain()
        v.tensor_scalar_sub(cmagic[:], cmagic[:], MAGIC)
        v.drain()
        v.tensor_scalar(cmagic[:], cmagic[:], 2.0 * (1.5 * 2.0 ** 52 - MAGIC), MAGIC,
                        ALU.mult, ALU.add)
        v.memset(onespad[:], 0.0)
        v.drain()
        v.memset(onespad[64:65, :], 1.0)
        v.memset(cEPSA[:], 64e-6)
        v.memset(cEPSL[:], 1e-5)
        return v.memset(cM16[:], -16.0)
    v_init = b.stepV(i_v)

    def cast_transpose(src_get, dst_fm, ndc=ND, waits=()):
        for tc in range(NT):
            tprev = b.cnt["T"]

            def cast(s, tc=tc, tprev=tprev):
                if tc == 0:
                    b.w(s, *waits)
                b.w(s, ("T", tprev))
                return s.activation(xcb[:, 0:ndc * 128], src_get(tc), AF.Copy)
            sv = b.stepS(cast)

            def tp(t, sv=sv):
                b.w(t, ("S", sv))
                for dc in range(ndc - 1):
                    t.transpose(pT[:, dc * 128:(dc + 1) * 128],
                                xcb[:, dc * 128:(dc + 1) * 128], ident[:])
                dc = ndc - 1
                return t.transpose(pT[:, dc * 128:(dc + 1) * 128],
                                   xcb[:, dc * 128:(dc + 1) * 128], ident[:])
            tv = b.stepT(tp)

            def evac(s, tc=tc, tv=tv):
                b.w(s, ("T", tv))
                for dc in range(ndc - 1):
                    s.copy(dst_fm[:, dc * T + tc * 128:dc * T + (tc + 1) * 128],
                           pT[:, dc * 128:(dc + 1) * 128])
                dc = ndc - 1
                return s.copy(dst_fm[:, dc * T + tc * 128:dc * T + (tc + 1) * 128],
                              pT[:, dc * 128:(dc + 1) * 128])
            b.stepS(evac)

    def load_w_strip(wkey, col0, ncols, dst=None, nrows=D, row0c=0, waits=()):
        dst = wsb if dst is None else dst
        nrc = nrows // 128
        return b.dma("gpsimd", "dg",
                     dst[:, 0:nrc * ncols].rearrange("p (c f) -> p c f", c=nrc),
                     w[wkey].rearrange("(c p) f -> p c f", p=128)
                     [:, row0c:row0c + nrc, col0:col0 + ncols],
                     waits=waits)

    def load_btile(bkey, col0, ncols, waits=()):
        return b.dma("gpsimd", "dg", btile[:, 0:ncols],
                     w[bkey][None, col0:col0 + ncols].broadcast_to([128, ncols]),
                     waits=waits)

    def stream_ln(buf):
        for tc in range(NT):
            ch = buf[:].rearrange("p (c f) -> p c f", c=NT)[:, tc, :]

            def stats(v, ch=ch):
                v.bn_stats(st1[:, 0:6], ch[:, 0:512])
                v.bn_stats(st1[:, 6:12], ch[:, 512:1024])
                v.drain()
                return v.bn_aggr(st2[:, 0:2], st1[:, 0:12])
            vv = b.stepV(stats)

            def sq(s, vv=vv):
                b.w(s, ("V", vv))
                return s.activation(st2[:, 2:3], st2[:, 1:2], AF.Sqrt, bias=cEPSL[:])
            sv = b.stepS(sq)

            def ab(v, sv=sv):
                b.w(v, ("S", sv))
                v.reciprocal(st2[:, 3:4], st2[:, 2:3])
                v.drain()
                v.tensor_scalar(st2[:, 4:5], st2[:, 0:1], st2[:, 3:4], -1.0,
                                ALU.mult, ALU.mult)
                v.drain()
                return v.tensor_copy(st2[:, 5:6], st2[:, 4:5])
            vv2 = b.stepV(ab)

            def apply(v, ch=ch, vv2=vv2):
                b.w(v, ("V", vv2))
                v.drain()
                return v.tensor_scalar(ch, ch, st2[:, 3:4], st2[:, 5:6],
                                       ALU.mult, ALU.add)
            b.stepV(apply)

    try:
        ck("init")
        cast_transpose(lambda tc: xbuf_c[:, tc, :], fmA,
                       waits=(("dl", dl_init), ("V", v_init)))
        fmA_done = b.cnt["S"]
        ck("fmA")

        s1ctx = ExitStack()
        GFm = b.sbuf("GFm", [16, T], BF16, s1ctx)
        bqg = b.sbuf("bqg", [16, 1], F32, s1ctx)
        dv_g = load_w_strip("L0_attn_Wq", D, 16)
        dv_b = b.dma("gpsimd", "dg", bqg[:], w["L0_attn_bq"][D:D + 16][:, None])

        def mm_acc(t, dst, lhs_of, rhs_of, n):
            for i in range(n - 1):
                t.matmul(dst, lhs_of(i), rhs_of(i), start=(i == 0), stop=False)
            return t.matmul(dst, lhs_of(n - 1), rhs_of(n - 1),
                            start=(n == 1), stop=True)

        for s4 in range(4):
            sprev = b.cnt["S"]

            def gmm(t, s4=s4, sprev=sprev):
                b.w(t, ("dg", dv_g), ("S", fmA_done))
                if s4 > 0:
                    b.w(t, ("S", sprev))
                return mm_acc(t, pS1[0:16, 0:512],
                              lambda dc: wsb[:, dc * 16:(dc + 1) * 16],
                              lambda dc, s4=s4: fmA[:, dc * T + s4 * 512:dc * T + (s4 + 1) * 512],
                              ND)
            tv = b.stepT(gmm)

            def gevac(s, s4=s4, tv=tv):
                b.w(s, ("T", tv), ("dg", dv_b))
                return s.activation(GFm[:, s4 * 512:(s4 + 1) * 512],
                                    pS1[0:16, 0:512], AF.Sigmoid, bias=bqg[:])
            b.stepS(gevac)
        ck("gate")

        QnF = b.sbuf("QnF", [128, 4 * T], BF16, s1ctx)
        KnF = b.sbuf("KnF", [128, 4 * T], BF16, s1ctx)
        Vp = b.sbuf("Vp", [128, NT * 520], BF16, s1ctx)
        ETb = b.sbuf("ETb", [128, 2 * 1024], BF16, s1ctx)
        stgQ = b.sbuf("stgQ", [128, T], BF16, s1ctx)
        stgQB = wsb[:, 0:T]
        stgY = b.sbuf("stgY", [65, T], BF16, s1ctx)
        NW = b.sbuf("NW", [128, 512], F32, s1ctx)
        glr = b.sbuf("glr", [65, T], BF16, s1ctx)
        dnr = b.sbuf("dnr", [65, 1024], F32, s1ctx)
        bcst = b.sbuf("bcst", [128, 1024], BF16, s1ctx)
        yTh = b.sbuf("yTh", [128, 4 * T], BF16, s1ctx)
        Vp_c = Vp[:].rearrange("p (c h f) -> p c h f", c=NT, h=8)

        def zero_misc(v):
            b.w(v, ("V", v_init))
            v.memset(bcst[:], 0.0)
            v.memset(stgQ[:], 0.0)
            return v.memset(Vp_c[:, :, :, 64:65], 1.0)
        v_misc = b.stepV(zero_misc)
        gscr = b.nc.dram_tensor("g_scr", [1024], BF16, kind="Internal").ap()
        HST = {"exp": [], "av": [], "smm_tv": [], "smm_last": [],
               "gs_last": [], "gev_last": [], "gw_last": [], "ydv_last": [],
               "gh": 0}

        def l0_proj_norm(half, wkey, bkey, nwkey, nwscale, dst_fm):
            col0 = half * 512
            wdv = load_w_strip(wkey, col0, 512, waits=[("T", b.cnt["T"])])
            bdv = load_btile(bkey, col0, 512, waits=[("V", b.cnt["V"])])
            nwdv = b.dma("gpsimd", "dg", NW[:],
                         w[nwkey][None, None, :].broadcast_to([128, 8, 64]),
                         waits=[("V", b.cnt["V"])])

            def nw_scale(v):
                b.w(v, ("dg", nwdv))
                return v.tensor_scalar_mul(NW[:], NW[:], nwscale)
            nw_vv = b.stepV(nw_scale)

            ev1_hist, tp_hist = [], []
            qnbs = [qnb, xcb[:, 0:512]]
            pend = []

            def emit_tp(tc, vv3):
                def tp(t, tc=tc, vv3=vv3, sprev=b.cnt["S"]):
                    b.w(t, ("V", vv3), ("S", sprev))
                    qn = qnbs[tc % 2]
                    for dc in range(3):
                        t.transpose(pT[:, dc * 128:(dc + 1) * 128],
                                    qn[:, dc * 128:(dc + 1) * 128], ident[:])
                    return t.transpose(pT[:, 3 * 128:4 * 128],
                                       qn[:, 3 * 128:4 * 128], ident[:])
                tv2 = b.stepT(tp)
                tp_hist.append(tv2)

                def ev2(s, tc=tc, tv2=tv2):
                    b.w(s, ("T", tv2))
                    for dc in range(3):
                        s.copy(dst_fm[:, dc * T + tc * 128:dc * T + (tc + 1) * 128],
                               pT[:, dc * 128:(dc + 1) * 128])
                    return s.copy(dst_fm[:, 3 * T + tc * 128:3 * T + (tc + 1) * 128],
                                  pT[:, 3 * 128:4 * 128])
                b.stepS(ev2)

            for tc in range(NT):
                po = pW[:, 0:512] if tc % 2 == 0 else pW[:, 512:1024]

                def pmm(t, tc=tc, po=po,
                        prev_v=(ev1_hist[-2] if len(ev1_hist) >= 2
                                else b.cnt["V"])):
                    b.w(t, ("dg", wdv), ("S", fmA_done), ("V", prev_v))
                    return mm_acc(t, po,
                                  lambda dc, tc=tc: fmA[:, dc * T + tc * 128:dc * T + (tc + 1) * 128],
                                  lambda dc: wsb[:, dc * 512:(dc + 1) * 512], ND)
                tv = b.stepT(pmm)
                if len(pend) >= 2:
                    emit_tp(*pend.pop(0))

                def ev1(v, tv=tv, po=po):
                    b.w(v, ("T", tv), ("dg", bdv), ("V", nw_vv))
                    return v.tensor_tensor(qch[:], po, btile[:, 0:512],
                                           ALU.add)
                vv1 = b.stepV(ev1)
                ev1_hist.append(vv1)

                def sq(s, vv1=vv1):
                    b.w(s, ("V", vv1))
                    return s.activation(qsq[:], qch[:], AF.Square)
                sv1 = b.stepS(sq)

                def red(v, sv1=sv1):
                    b.w(v, ("S", sv1))
                    for hh in range(7):
                        v.tensor_reduce(st1[:, hh:hh + 1],
                                        qsq[:, hh * 64:(hh + 1) * 64],
                                        axis=AX, op=ALU.add)
                    return v.tensor_reduce(st1[:, 7:8], qsq[:, 7 * 64:8 * 64],
                                           axis=AX, op=ALU.add)
                vv2 = b.stepV(red)

                def rsq(s, vv2=vv2):
                    b.w(s, ("V", vv2))
                    return s.activation(st2[:, 0:8], st1[:, 0:8], AF.Sqrt,
                                        bias=cEPSA[:])
                sv2 = b.stepS(rsq)

                def nrm(v, tc=tc, sv2=sv2,
                        tpprev=(tp_hist[-1] if tp_hist else None)):
                    b.w(v, ("S", sv2))
                    if tpprev is not None:
                        b.w(v, ("T", tpprev))
                    v.reciprocal(st3[:, 0:8], st2[:, 0:8])
                    v.drain()
                    v.tensor_copy(qsq[:].rearrange("p (h f) -> p h f", h=8),
                                  st3[:, 0:8][:, :, None].broadcast_to([128, 8, 64]))
                    v.drain()
                    v.tensor_mul(qsq[:], qsq[:], NW[:])
                    v.drain()
                    return v.tensor_mul(qnbs[tc % 2][:, 0:512], qch[:], qsq[:])
                vv3 = b.stepV(nrm)
                pend.append((tc, vv3))
            while pend:
                emit_tp(*pend.pop(0))

        def l0_vproj(half):
            col0 = half * 512
            wdv = load_w_strip("L0_attn_Wv", col0, 512, waits=[("T", b.cnt["T"])])
            bdv = load_btile("L0_attn_bv", col0, 512, waits=[("V", b.cnt["V"])])
            for tc in range(NT):
                prev_v = b.cnt["V"]

                def pmm(t, tc=tc, prev_v=prev_v):
                    b.w(t, ("dg", wdv), ("S", fmA_done), ("V", prev_v))
                    return mm_acc(t, pW[:, 0:512],
                                  lambda dc, tc=tc: fmA[:, dc * T + tc * 128:dc * T + (tc + 1) * 128],
                                  lambda dc: wsb[:, dc * 512:(dc + 1) * 512], ND)
                tv = b.stepT(pmm)

                def ev(v, tc=tc, tv=tv):
                    b.w(v, ("T", tv), ("dg", bdv), ("V", v_misc))
                    return v.scalar_tensor_tensor(
                        Vp_c[:, tc, :, 0:64],
                        pW[:, 0:512].rearrange("p (h f) -> p h f", h=8),
                        1.0,
                        btile[:, 0:512].rearrange("p (h f) -> p h f", h=8),
                        ALU.mult, ALU.add)
                b.stepV(ev)

        def l0_heads(half):
            def zB(v, tprev=b.cnt["T"]):
                b.w(v, ("T", tprev))
                return v.memset(stgQB[0:64, :], 0.0)
            b.stepV(zB)
            for hh in range(8):
                h = half * 8 + hh
                par = h % 2
                pair = hh // 2
                gh = HST["gh"]
                stq = stgQ if par == 0 else stgQB
                dsv = b.dma("sync", "ds",
                            stq[par * 64:par * 64 + 64, :],
                            QnF[par * 64:par * 64 + 64, pair * T:(pair + 1) * T],
                            waits=([("T", HST["smm_last"][gh - 2])] if hh >= 2
                                   else [("S", b.cnt["S"]), ("T", b.cnt["T"]),
                                         ("V", b.cnt["V"])]))
                gdv = b.dma("sync", "ds", glr[64:65, :], GFm[h:h + 1, :],
                            waits=([("V", HST["gs_last"][2 * gh - 1])] if gh >= 1
                                   else [("V", b.cnt["V"])]))

                for qh in range(2):
                    q0 = qh * 1024
                    pend_av = None
                    for kc in range(NT):
                        i = len(HST["exp"])
                        pb = pBig[:, 0:1024] if i % 2 == 0 else pBig[:, 1024:2048]
                        eth = ETb[:, 0:1024] if i % 2 == 0 else ETb[:, 1024:2048]

                        def smm(t, kc=kc, pb=pb, dsv=dsv, pair=pair, q0=q0,
                                stq=stq,
                                ex2=(HST["exp"][i - 2] if i >= 2 else None)):
                            if kc == 0:
                                b.w(t, ("ds", dsv))
                            if ex2 is not None:
                                b.w(t, ("S", ex2))
                            t.matmul(pb[:, 0:512],
                                     KnF[:, pair * T + kc * 128:pair * T + (kc + 1) * 128],
                                     stq[:, q0:q0 + 512], start=True, stop=True)
                            return t.matmul(pb[:, 512:1024],
                                            KnF[:, pair * T + kc * 128:pair * T + (kc + 1) * 128],
                                            stq[:, q0 + 512:q0 + 1024],
                                            start=True, stop=True)
                        tv = b.stepT(smm)
                        HST["smm_tv"].append(tv)

                        def expv(s, tv=tv, pb=pb, eth=eth,
                                 av2=(HST["av"][i - 2] if i >= 2 else None)):
                            b.w(s, ("T", tv))
                            if av2 is not None:
                                b.w(s, ("T", av2))
                            return s.activation(eth, pb, AF.Exp, bias=cM16[:])
                        sv = b.stepS(expv)
                        HST["exp"].append(sv)

                        def mk_avmm(kc, sv, eth, hh=hh):
                            def avmm(t, hh=hh, kc=kc, sv=sv, eth=eth,
                                     gv_prev=(HST["gev_last"][-1]
                                              if (kc == 0 and HST["gev_last"])
                                              else None)):
                                b.w(t, ("S", sv))
                                if gv_prev is not None:
                                    b.w(t, ("V", gv_prev))
                                t.matmul(pW[0:65, 0:512], Vp_c[:, kc, hh, :],
                                         eth[:, 0:512],
                                         start=(kc == 0), stop=False)
                                return t.matmul(pW[0:65, 512:1024],
                                                Vp_c[:, kc, hh, :],
                                                eth[:, 512:1024],
                                                start=(kc == 0),
                                                stop=(kc == NT - 1))
                            HST["av"].append(b.stepT(avmm))

                        if pend_av is not None:
                            mk_avmm(*pend_av)
                        pend_av = (kc, sv, eth)
                    mk_avmm(*pend_av)

                    def den_ev(s, avt=HST["av"][-1],
                               gs_prev=(HST["gs_last"][-1] if HST["gs_last"]
                                        else None)):
                        b.w(s, ("T", avt))
                        if gs_prev is not None:
                            b.w(s, ("V", gs_prev))
                        return s.copy(dnr[64:65, 0:1024], pW[64:65, 0:1024])
                    sv_d = b.stepS(den_ev)

                    def gs_mk(v, q0=q0, sv_d=sv_d, gdv=gdv,
                              gw_prev=(HST["gw_last"][-1] if HST["gw_last"]
                                       else None)):
                        b.w(v, ("S", sv_d), ("ds", gdv))
                        if gw_prev is not None:
                            b.w(v, ("ds", gw_prev))
                        v.reciprocal(dnr[64:65, 0:1024], dnr[64:65, 0:1024])
                        v.drain()
                        return v.tensor_tensor(bcst[64:65, 0:1024],
                                               dnr[64:65, 0:1024],
                                               glr[64:65, q0:q0 + 1024], ALU.mult)
                    vv_gs = b.stepV(gs_mk)
                    HST["gs_last"].append(vv_gs)

                    HST["gw_last"].append(
                        b.dma("sync", "ds", gscr[None, :], bcst[64:65, 0:1024],
                              waits=[("V", vv_gs)]))
                    bcb = b.dma("sync", "ds", bcst[0:64, 0:1024],
                                gscr[None, :].broadcast_to([64, 1024]),
                                waits=([("V", HST["gev_last"][-1])]
                                       if HST["gev_last"] else []) +
                                [("ds", HST["gw_last"][-1])] +
                                ([("ds", HST["ydv_last"][-1])]
                                 if HST["ydv_last"] else []))

                    for s2 in range(2):
                        c0 = q0 + s2 * 512

                        def gev(v, c0=c0, s2=s2, bcb=bcb):
                            b.w(v, ("ds", bcb))
                            return v.tensor_mul(stgY[0:64, c0:c0 + 512],
                                                pW[0:64, s2 * 512:(s2 + 1) * 512],
                                                bcst[0:64, s2 * 512:(s2 + 1) * 512])
                        HST["gev_last"].append(b.stepV(gev))

                ydv = b.dma("sync", "ds",
                            yTh[par * 64:par * 64 + 64, (hh // 2) * T:(hh // 2 + 1) * T],
                            stgY[0:64, :],
                            waits=[("V", b.cnt["V"])])
                HST["ydv_last"].append(ydv)
                HST["smm_last"].append(HST["smm_tv"][-1])
                HST["gh"] += 1
            return b.cnt["ds"]

        for half in range(2):
            l0_proj_norm(half, "L0_attn_Wq", "L0_attn_bq", "L0_attn_qn_w", 1.0, QnF)
            l0_proj_norm(half, "L0_attn_Wk", "L0_attn_bk", "L0_attn_kn_w", 8.0, KnF)
            ck("proj%d" % half)
            l0_vproj(half)
            ck("vproj%d" % half)
            yT_dma = l0_heads(half)
            ck("heads%d" % half)

            for strip in range(2):
                c0 = strip * 512
                wdv = load_w_strip("L0_attn_Wo", c0, 512, row0c=4 * half, nrows=512,
                                   waits=[("T", b.cnt["T"])])
                bdv = load_btile("L0_attn_bo", c0, 512, waits=[("V", b.cnt["V"])])
                for tc in range(NT):
                    po = pW[:, 0:512] if tc % 2 == 0 else pW[:, 512:1024]
                    prev_v = b.cnt["V"]

                    def omm(t, tc=tc, po=po, wdv=wdv, prev_v=prev_v, yT_dma=yT_dma):
                        b.w(t, ("dg", wdv), ("ds", yT_dma), ("V", prev_v))
                        return mm_acc(t, po,
                                      lambda dc, tc=tc: yTh[:, dc * T + tc * 128:dc * T + (tc + 1) * 128],
                                      lambda dc: wsb[:, dc * 512:(dc + 1) * 512], 4)
                    tv = b.stepT(omm)

                    def oev(v, tc=tc, tv=tv, po=po, c0=c0, bdv=bdv, half=half):
                        b.w(v, ("T", tv), ("dg", bdv))
                        xs = xbuf_c[:, tc, c0:c0 + 512]
                        if half == 0:
                            v.tensor_tensor(qch[:], po, btile[:, 0:512], ALU.add)
                            v.drain()
                            return v.tensor_tensor(xs, xs, qch[:], ALU.add)
                        else:
                            return v.tensor_tensor(xs, xs, po, ALU.add)
                    b.stepV(oev)

        ck("outproj")
        stream_ln(xbuf)
        ck("stage1")

        if n_stages >= 2:
            b.barrier()
            s1ctx.close()
            s1ctx = ExitStack()
            emit_l0_ffn(b, w, ck, dict(
                xbuf=xbuf, xbuf_c=xbuf_c, fmA=fmA, wsb=wsb, btile=btile,
                xcb=xcb, cmagic=cmagic, ident=ident, identf=identf, onespad=onespad,
                st1=st1, st2=st2, st3=st3, qch=qch, qsq=qsq, qnb=qnb,
                pT=pT, pS1=pS1, pW=pW, pBig=pBig, cEPSL=cEPSL,
                stream_ln=stream_ln, mm_acc=mm_acc))
        ck("stage2")

        if n_stages >= 3:
            b.barrier()
            emit_l1_attn(b, w, ck, dict(
                xbuf=xbuf, xbuf_c=xbuf_c, fmA=fmA, wsb=wsb, btile=btile,
                xcb=xcb, cmagic=cmagic, ident=ident, identf=identf,
                onespad=onespad, st1=st1, st2=st2, st3=st3, qch=qch, qsq=qsq,
                qnb=qnb, pT=pT, pS1=pS1, pW=pW, pBig=pBig, cEPSL=cEPSL,
                stream_ln=stream_ln, mm_acc=mm_acc))
        ck("stage3")

        if n_stages >= 4:
            b.barrier()
            emit_l1_ffn(b, w, ck, dict(
                xbuf=xbuf, xbuf_c=xbuf_c, fmA=fmA, wsb=wsb, btile=btile,
                xcb=xcb, ident=ident, st1=st1, st2=st2, qch=qch,
                pT=pT, pS1=pS1, pW=pW, pBig=pBig,
                stream_ln=stream_ln, mm_acc=mm_acc,
                cast_transpose=cast_transpose))
        ck("stage4")
    except _Stop:
        pass

    b.flush()
    s1ctx.close()
    if os.environ.get("KB_DUMPCNT"):
        print("CNT:", sorted(b.cnt.items()))
    vdone = b.cnt["V"]
    sdone = b.cnt["S"]
    odv = b.dma("sync", "do",
                out_ap.rearrange("(c p) f -> p c f", p=128), xbuf_c,
                waits=[("V", vdone), ("S", sdone)])

    def fin(e):
        e.wait_ge(sem["do"], odv)
    b.blk.sync(fin)


def emit_l0_ffn(b, w, ck, env):
    sem = b.sem
    nc = b.nc
    xbuf_c = env["xbuf_c"]
    fmA, wsb, xcb = env["fmA"], env["wsb"], env["xcb"]
    cmagic, ident, ones128 = env["cmagic"], env["ident"], env["onespad"]
    identf = env["identf"]
    st1 = env["st1"]
    qch, qsq = env["qch"], env["qsq"]
    pT, pS1, pW, pBig = env["pT"], env["pS1"], env["pW"], env["pBig"]
    stream_ln, mm_acc = env["stream_ln"], env["mm_acc"]
    xbuf = env["xbuf"]

    NG, TG = 8, 256
    NF = 32

    DBG = False
    wkind = "ExternalOutput" if DBG else "Internal"
    wq1t = nc.dram_tensor("wq1t_scr", [NF, 128, 1024], BF16, kind=wkind).ap()
    wq2t = nc.dram_tensor("wq2t_scr", [NF, 128, 1024], BF16, kind=wkind).ap()
    sclh = nc.dram_tensor("scl_scr", [8], F32, kind="Internal").ap()

    with ExitStack() as sc:
        ttile = b.sbuf("ttile", [128, 2 * 1024], BF16, sc)
        ttile2 = b.sbuf("ttile2", [128, 2 * 1024], BF16, sc)
        macc = b.sbuf("macc", [128, 32], F32, sc)
        mrow = b.sbuf("mrow", [1, 128], F32, sc)
        mtv = b.sbuf("mtv", [1, 8], F32, sc)
        m1c = b.sbuf("m1c", [128, 1], F32, sc)
        thrc = b.sbuf("thrc", [128, 1], F32, sc)
        m2d = b.sbuf("m2d", [128, 1], F32, sc)
        xfs = b.sbuf("xfs", [128, 1024], F32, sc)
        dqt = b.sbuf("dqt", [128, 16], F32, sc)
        dq16 = b.sbuf("dq16", [16, 128], F32, sc)
        dqpad = b.sbuf("dqpad", [128, T], BF16, sc)
        Br = b.sbuf("Br", [128, T], BF16, sc)
        cw = b.sbuf("cw", [128, 32 * 3], F32, sc)
        cb = b.sbuf("cb", [128, 32], F32, sc)
        shset = b.sbuf("shset", [128, NF * TG], F32, sc)
        sqset = b.sbuf("sqset", [128, NF * TG], BF16, sc)
        hb = b.sbuf("hb", [128, 2 * 258], F32, sc)
        s2pad = b.sbuf("s2pad", [128, 256], BF16, sc)
        s2t = b.sbuf("s2t", [128, 2], F32, sc)
        s2r = b.sbuf("s2r", [2, 128], F32, sc)
        Bs2 = b.sbuf("Bs2", [128, 256], F32, sc)
        r2t = b.sbuf("r2t", [128, 2], F32, sc)
        mxt = b.sbuf("mxt", [128, 4], F32, sc)
        gmax = b.sbuf("gmax", [128, 256], F32, sc)
        gmin = b.sbuf("gmin", [128, 256], F32, sc)
        dbgm = b.sbuf("dbgm", [128, 164], F32, sc)
        wq1b = b.sbuf("wq1b", [128, 4 * 1024], BF16, sc)
        wq2b = b.sbuf("wq2b", [128, 2 * 1024], BF16, sc)

        w1ap = w["L0_ffn_w1"].rearrange("(c p) f -> p c f", p=128)
        w2ap = w["L0_ffn_w2"].rearrange("(c p) f -> p c f", p=128)

        def zero2(v):
            v.memset(dqpad[:], 0.0)
            v.memset(s2pad[:], 0.0)
            v.memset(hb[:], 0.0)
            return v.memset(macc[:], 0.0)
        vz = b.stepV(zero2)

        def mean_pass(ap_chunks, col_ofs, n):
            v_after = []
            for i in range(n):
                qtr = i % 4
                eng, sn = (("sync", "ds") if i % 2 == 0 else ("gpsimd", "dg"))
                dv = b.dma(eng, sn, shset[:, qtr * 1024:(qtr + 1) * 1024],
                           ap_chunks(i),
                           waits=([("V", v_after[i - 4])] if i >= 4
                                  else [("V", b.cnt["V"])]))

                def red(v, i=i, qtr=qtr, dv=dv, sn=sn):
                    b.w(v, (sn, dv))
                    return v.tensor_reduce(macc[:, (col_ofs + i) % 32:(col_ofs + i) % 32 + 1],
                                           shset[:, qtr * 1024:(qtr + 1) * 1024],
                                           axis=AX, op=ALU.add,
                                           apply_absolute_value=True)
                v_after.append(b.stepV(red))

        def finish_mean(dst_col_m, dst_col_thr, denom, want_thr):
            def r2_(v):
                v.drain()
                v.tensor_reduce(st1[:, 0:1], macc[:, 0:32], axis=AX, op=ALU.add)
                v.drain()
                return v.tensor_copy(st1[:, 1:2], st1[:, 0:1])
            vv = b.stepV(r2_)

            def tpm(t, vv=vv, sprev=b.cnt["S"]):
                b.w(t, ("V", vv), ("S", sprev))
                return t.transpose(pS1[0:1, 0:128], st1[:, 0:1], identf[:])
            tv = b.stepT(tpm)

            def evm(s, tv=tv):
                b.w(s, ("T", tv))
                return s.copy(mrow[:], pS1[0:1, 0:128])
            sv = b.stepS(evm)

            def fin(v, sv=sv, denom=denom):
                b.w(v, ("S", sv))
                v.tensor_reduce(mtv[0:1, 0:1], mrow[:], axis=AX, op=ALU.add)
                v.drain()
                v.tensor_scalar_mul(mtv[0:1, 0:1], mtv[0:1, 0:1], 1.0 / denom)
                v.drain()
                v.tensor_scalar_max(mtv[0:1, dst_col_m:dst_col_m + 1],
                                    mtv[0:1, 0:1], 1e-5)
                v.drain()
                if want_thr:
                    v.tensor_scalar_mul(mtv[0:1, dst_col_thr:dst_col_thr + 1],
                                        mtv[0:1, dst_col_m:dst_col_m + 1], 0.5)
                else:
                    v.tensor_scalar_mul(mtv[0:1, dst_col_thr:dst_col_thr + 1],
                                        mtv[0:1, dst_col_m:dst_col_m + 1], 1.0 / 127.0)
                v.drain()
                v.reciprocal(mtv[0:1, 6:7], mtv[0:1, dst_col_m:dst_col_m + 1])
                v.drain()
                return v.tensor_copy(mtv[0:1, 7:8], mtv[0:1, 0:1])
            return b.stepV(fin)

        mean_pass(lambda i: w1ap[:, i, :], 0, 32)
        vvm = finish_mean(1, 2, 4096.0 * 1024.0, True)

        def dbgsnap(v, vvm=vvm):
            b.w(v, ("V", vvm))
            v.tensor_copy(dbgm[:, 0:32], macc[:, 0:32])
            v.tensor_copy(dbgm[:, 32:33], st1[:, 0:1])
            return v.tensor_copy(dbgm[0:1, 33:161], mrow[:])
        b.stepV(dbgsnap)
        b.dma("sync", "ds", sclh[None, :], mtv[0:1, :],
              waits=[("V", b.settleV(waits=(("V", vvm),)))])
        hb1 = b.cnt["ds"]
        bdv1 = b.dma("gpsimd", "dg", m1c[:],
                     sclh[None, 1:2].broadcast_to([128, 1]), waits=[("ds", hb1)])
        bdv2 = b.dma("gpsimd", "dg", thrc[:],
                     sclh[None, 6:7].broadcast_to([128, 1]), waits=[("ds", hb1)])

        def tern_pass(ap_of, n, scr, dst_of, scale_dv):
            v_after, s_after, d_spill = [], [], []
            for i in range(n):
                qtr = i % 4
                half = i % 2
                eng, sn = (("sync", "ds") if i % 2 == 0 else ("gpsimd", "dg"))
                dv = b.dma(eng, sn, shset[:, qtr * 1024:(qtr + 1) * 1024],
                           ap_of(i),
                           waits=([("V", v_after[i - 4])] if i >= 4
                                  else [("V", b.cnt["V"]), ("S", b.cnt["S"])]))
                tt = ttile[:, half * 1024:(half + 1) * 1024] if False else None

                def tern(v, qtr=qtr, half=half, dv=dv, sn=sn,
                         tprev=b.cnt["T"], scale_dv=scale_dv):
                    b.w(v, (sn, dv), ("dg", scale_dv), ("T", tprev))
                    v.tensor_scalar(xfs[:], shset[:, qtr * 1024:(qtr + 1) * 1024],
                                    thrc[:], -1.0, ALU.mult, ALU.max)
                    v.tensor_scalar(xfs[:], xfs[:], 1.0, cmagic[:],
                                    ALU.min, ALU.add)
                    return v.tensor_scalar(ttile[:, half * 1024:(half + 1) * 1024],
                                           xfs[:], cmagic[:], None,
                                           ALU.subtract)
                vv2 = b.stepV(tern)
                v_after.append(vv2)

                def tpp(t, vv2=vv2, half=half,
                        sprev=(s_after[i - 1] if i >= 1 else b.cnt["S"])):
                    b.w(t, ("V", vv2), ("S", sprev))
                    for dc in range(7):
                        t.transpose(pT[:, dc * 128:(dc + 1) * 128],
                                    ttile[:, half * 1024 + dc * 128:half * 1024 + (dc + 1) * 128],
                                    ident[:])
                    return t.transpose(pT[:, 7 * 128:8 * 128],
                                       ttile[:, half * 1024 + 7 * 128:half * 1024 + 8 * 128],
                                       ident[:])
                tv = b.stepT(tpp)

                def evt(s, tv=tv, half=half,
                        dprev=(d_spill[i - 2] if i >= 2 else b.cnt["ds"])):
                    b.w(s, ("T", tv), ("ds", dprev))
                    for dc in range(7):
                        s.copy(ttile2[:, half * 1024 + dc * 128:half * 1024 + (dc + 1) * 128],
                               pT[:, dc * 128:(dc + 1) * 128])
                    return s.copy(ttile2[:, half * 1024 + 7 * 128:half * 1024 + 8 * 128],
                                  pT[:, 7 * 128:8 * 128])
                sv2 = b.stepS(evt)
                s_after.append(sv2)
                d_spill.append(b.dma("sync", "ds", dst_of(i),
                                     ttile2[:, half * 1024:(half + 1) * 1024],
                                     waits=[("S", sv2)]))

        tern_pass(lambda i: w1ap[:, i, :], 32, wq1t, lambda i: wq1t[i], bdv2)
        w1_spill_done = b.cnt["ds"]

        def zero_macc(v, vprev=b.cnt["V"]):
            b.w(v, ("V", vprev))
            return v.memset(macc[:], 0.0)
        b.stepV(zero_macc)
        mean_pass(lambda i: w2ap[:, i // 4, (i % 4) * 1024:(i % 4 + 1) * 1024], 0, 32)
        vvm2 = finish_mean(3, 4, 1024.0 * 4096.0, False)
        def thr2(v, vvm2=vvm2):
            b.w(v, ("V", vvm2))
            return v.tensor_scalar_mul(mtv[0:1, 5:6], mtv[0:1, 3:4], 0.5)
        vvt2 = b.stepV(thr2)
        b.dma("sync", "ds", sclh[None, :], mtv[0:1, :],
              waits=[("V", b.settleV(waits=(("V", vvt2),))), ("dg", b.cnt["dg"])])
        hb2 = b.cnt["ds"]
        bdv3 = b.dma("gpsimd", "dg", m2d[:],
                     sclh[None, 4:5].broadcast_to([128, 1]), waits=[("ds", hb2)])
        bdv4 = b.dma("gpsimd", "dg", thrc[:],
                     sclh[None, 6:7].broadcast_to([128, 1]),
                     waits=[("ds", hb2), ("V", b.cnt["V"])])
        tern_pass(lambda i: w2ap[:, i // 4, (i % 4) * 1024:(i % 4 + 1) * 1024],
                  32, wq2t,
                  lambda i: wq2t[:].rearrange("c p f -> p c f")[:, (i % 4) * 8:(i % 4) * 8 + 8,
                                                               (i // 4) * 128:(i // 4 + 1) * 128]
                  if False else
                  wq2t.rearrange("c p f -> p c f", c=NF)[:, (i % 4) * 8:(i % 4) * 8 + 8,
                                                         (i // 4) * 128:(i // 4 + 1) * 128],
                  bdv4)

        cdv = b.dma("sync", "ds", cw[:].rearrange("p (c t) -> p c t", c=32),
                    w["L0_ffn_conv_w"].rearrange("(c p) u t -> p c (u t)", p=128),
                    slow=True)
        cbdv = b.dma("sync", "ds", cb[:],
                     w["L0_ffn_conv_b"].rearrange("(c p) -> p c", p=128),
                     slow=True)

        def cwm_mk(v, cdv=cdv):
            b.w(v, ("ds", cdv), ("dg", bdv1))
            return v.tensor_scalar_mul(cw[:], cw[:], m1c[:])
        v_cwm = b.stepV(cwm_mk)

        for tc in range(NT):
            ch = xbuf_c[:, tc, :]
            tprev = b.cnt["T"]

            def q1(v, ch=ch, tc=tc, tprev=tprev):
                b.w(v, ("T", tprev))
                v.tensor_reduce(st1[:, 0:1], ch, axis=AX, op=ALU.max,
                                apply_absolute_value=True)
                v.drain()
                v.tensor_scalar_max(st1[:, 1:2], st1[:, 0:1], 1e-5)
                v.drain()
                v.tensor_scalar_mul(dqt[:, tc:tc + 1], st1[:, 1:2], 1.0 / 127.0)
                v.reciprocal(st1[:, 2:3], st1[:, 1:2])
                v.drain()
                v.tensor_scalar_mul(st1[:, 3:4], st1[:, 2:3], 127.0)
                v.drain()
                v.tensor_scalar(xfs[:], ch, st1[:, 3:4], cmagic[:], ALU.mult, ALU.add)
                v.drain()
                return v.tensor_scalar(xcb[:], xfs[:], cmagic[:], None, ALU.subtract)
            sv = b.stepV(q1)

            def tpq(t, sv=sv, sprev=b.cnt["S"]):
                b.w(t, ("V", sv), ("S", sprev))
                for dc in range(7):
                    t.transpose(pT[:, dc * 128:(dc + 1) * 128],
                                xcb[:, dc * 128:(dc + 1) * 128], ident[:])
                return t.transpose(pT[:, 7 * 128:8 * 128],
                                   xcb[:, 7 * 128:8 * 128], ident[:])
            tv = b.stepT(tpq)

            def evq(s, tc=tc, tv=tv):
                b.w(s, ("T", tv))
                for dc in range(7):
                    s.copy(fmA[:, dc * T + tc * 128:dc * T + (tc + 1) * 128],
                           pT[:, dc * 128:(dc + 1) * 128])
                return s.copy(fmA[:, 7 * T + tc * 128:7 * T + (tc + 1) * 128],
                              pT[:, 7 * 128:8 * 128])
            b.stepS(evq)


        def tpd(t, vprev=b.cnt["V"], sprev=b.cnt["S"]):
            b.w(t, ("V", vprev), ("S", sprev))
            return t.transpose(pS1[0:16, 0:128], dqt[:], identf[:])
        tv = b.stepT(tpd)

        def evd(s, tv=tv):
            b.w(s, ("T", tv))
            return s.copy(dq16[:], pS1[0:16, 0:128])
        sv = b.stepS(evd)
        for r in range(16):
            ddv = b.dma("gpsimd", "dg", dqpad[64:65, r * 128:(r + 1) * 128],
                        dq16[r:r + 1, :], waits=([("S", sv)] if r == 0 else []))
        for s4 in range(4):
            def brm(t, s4=s4, ddv=ddv, vprev=b.cnt["V"]):
                b.w(t, ("dg", ddv), ("V", vprev))
                return t.matmul(pS1[:, 0:512], ones128[:],
                                dqpad[:, s4 * 512:(s4 + 1) * 512],
                                start=True, stop=True)
            tvb = b.stepT(brm)

            def bre(v, s4=s4, tvb=tvb):
                b.w(v, ("T", tvb))
                return v.tensor_copy(Br[:, s4 * 512:(s4 + 1) * 512], pS1[:, 0:512])
            b.stepV(bre)

        g1hist, g2hist = [], []
        w2_spill_done = b.cnt["ds"]
        hev_last = [None] * NG
        hev_hist = []

        def emit_m1(tg):
            t0 = tg * TG
            lo = max(t0 - 1, 0)
            hi = min(t0 + TG + 1, T)
            span = hi - lo
            hofs = 1 - (t0 - lo)
            if tg == NG - 1:
                def zg(v, vprev=b.cnt["V"]):
                    b.w(v, ("V", vprev))
                    v.memset(hb[:, 257:258], 0.0)
                    return v.memset(hb[:, 258 + 257:258 + 258], 0.0)
                b.stepV(zg)
            for dffc in range(NF):
                half = dffc % 2
                g1 = len(g1hist)
                qtr = g1 % 4
                hbv = hb[:, half * 258:half * 258 + 258]
                pmm1 = pS1[:, 0:258] if half == 0 else pW[:, 0:258]
                w1dv = b.dma("sync", "ds", wq1b[:, qtr * 1024:(qtr + 1) * 1024],
                             wq1t[dffc],
                             waits=[("T", g1hist[g1 - 4])] if g1 >= 4
                             else [("ds", w1_spill_done)])

                def m1f(t, tg=tg, dffc=dffc, qtr=qtr, w1dv=w1dv, lo=lo, span=span,
                        pmm1=pmm1,
                        vprev=(hev_hist[-2] if len(hev_hist) >= 2
                               else b.cnt["V"])):
                    b.w(t, ("ds", w1dv), ("V", vprev))
                    return mm_acc(t, pmm1[:, 0:span],
                                  lambda dc, qtr=qtr: wq1b[:, qtr * 1024 + dc * 128:
                                                           qtr * 1024 + (dc + 1) * 128],
                                  lambda dc, lo=lo, span=span: fmA[:, dc * T + lo:dc * T + lo + span],
                                  ND)
                tv1 = b.stepT(m1f)
                g1hist.append(tv1)

                def hev(v, tg=tg, half=half, tv1=tv1, pmm1=pmm1, lo=lo, span=span,
                        hofs=hofs, hbv=hbv):
                    b.w(v, ("T", tv1))
                    return v.tensor_tensor(hbv[:, hofs:hofs + span], pmm1[:, 0:span],
                                           Br[:, lo:lo + span], ALU.mult)
                vv1 = b.stepV(hev)
                hev_last[tg] = vv1
                hev_hist.append(vv1)

                def conv(v, hbv=hbv, dffc=dffc, vv1=vv1, sprev=b.cnt["S"]):
                    b.w(v, ("V", vv1), ("S", sprev))
                    v.tensor_scalar(qch[:, 0:256], hbv[:, 0:256],
                                    cw[:, dffc * 3:dffc * 3 + 1], None, ALU.mult)
                    v.scalar_tensor_tensor(qch[:, 0:256], hbv[:, 1:257],
                                           cw[:, dffc * 3 + 1:dffc * 3 + 2],
                                           qch[:, 0:256], ALU.mult, ALU.add)
                    return v.scalar_tensor_tensor(qch[:, 0:256], hbv[:, 2:258],
                                                  cw[:, dffc * 3 + 2:dffc * 3 + 3],
                                                  qch[:, 0:256], ALU.mult, ALU.add)
                vv2 = b.stepV(conv)

                def silu(s, dffc=dffc, vv2=vv2, cbdv=cbdv):
                    b.w(s, ("V", vv2), ("ds", cbdv))
                    return s.activation(shset[:, dffc * TG:(dffc + 1) * TG],
                                        qch[:, 0:256], AF.Silu,
                                        bias=cb[:, dffc:dffc + 1])
                b.stepS(silu)

        def emit_q2(tg):
            pQ = pS1 if os.environ.get("KB_S2PS1") else pBig

            def gfold(v, sprev=b.cnt["S"],
                      m2prev=(g2hist[-1] if g2hist else None)):
                b.w(v, ("S", sprev))
                if m2prev is not None:
                    b.w(v, ("T", m2prev))
                for op, dst in ((ALU.max, gmax), (ALU.min, gmin)):
                    v.tensor_tensor(dst[:], shset[:, 0:256], shset[:, 256:512], op)
                    for i in range(2, NF):
                        v.tensor_tensor(dst[:], dst[:],
                                        shset[:, i * 256:(i + 1) * 256], op)
                v.drain()
                v.tensor_scalar_mul(gmin[:], gmin[:], -1.0)
                v.drain()
                return v.tensor_tensor(gmax[:], gmax[:], gmin[:], ALU.max)
            gv = b.stepV(gfold)

            def tpg(t, gv=gv, sprev=b.cnt["S"]):
                b.w(t, ("V", gv), ("S", sprev))
                t.transpose(pQ[:, 0:128], gmax[:, 0:128], identf[:])
                return t.transpose(pQ[:, 128:256], gmax[:, 128:256], identf[:])
            tv = b.stepT(tpg)

            def mxred(v, tv=tv):
                b.w(v, ("T", tv))
                v.tensor_reduce(mxt[:, 0:1], pQ[:, 0:128], axis=AX, op=ALU.max)
                v.drain()
                v.tensor_reduce(mxt[:, 1:2], pQ[:, 128:256], axis=AX, op=ALU.max)
                v.drain()
                v.tensor_scalar_max(mxt[:, 0:2], mxt[:, 0:2], 1e-5)
                v.drain()
                v.tensor_scalar(r2t[:, 0:2], mxt[:, 0:2], m2d[:], None, ALU.mult)
                v.reciprocal(s2t[:, 0:2], mxt[:, 0:2])
                v.drain()
                return v.tensor_scalar_mul(s2t[:, 0:2], s2t[:, 0:2], 127.0)
            vv = b.stepV(mxred)

            def tps(t, vv=vv):
                b.w(t, ("V", vv))
                return t.transpose(pQ[0:2, 256:384], s2t[:, 0:2], identf[:])
            tv2 = b.stepT(tps)

            def evs(s, tv2=tv2):
                b.w(s, ("T", tv2))
                return s.copy(s2r[:], pQ[0:2, 256:384])
            sv = b.stepS(evs)
            for r in range(2):
                sdv = b.dma("gpsimd", "dg", s2pad[64:65, r * 128:(r + 1) * 128],
                            s2r[r:r + 1, :], waits=([("S", sv)] if r == 0 else []))

            def bs2m(t, sdv=sdv, vprev=b.cnt["V"]):
                b.w(t, ("dg", sdv), ("V", vprev))
                return t.matmul(pQ[:, 256:512], ones128[:], s2pad[:],
                                start=True, stop=True)
            tvb = b.stepT(bs2m)

            def bs2e(v, tvb=tvb):
                b.w(v, ("T", tvb))
                return v.tensor_copy(Bs2[:], pQ[:, 256:512])
            vvb = b.stepV(bs2e)

            def qmul(v, vvb=vvb):
                b.w(v, ("V", vvb))
                v.drain()
                return v.tensor_tensor(
                    shset[:].rearrange("p (c f) -> p c f", c=NF),
                    shset[:].rearrange("p (c f) -> p c f", c=NF),
                    Bs2[:][:, None, :].broadcast_to([128, NF, TG]),
                    ALU.mult)
            vvq = b.stepV(qmul)

            def qrnd(v, vvq=vvq):
                b.w(v, ("V", vvq))
                v.tensor_scalar(shset[:], shset[:], cmagic[:], None, ALU.add)
                return v.tensor_scalar(sqset[:], shset[:], cmagic[:], None,
                                       ALU.subtract)
            return b.stepV(qrnd)

        def emit_m2(tg, gq):
            for dffc in range(NF):
                half = dffc % 2
                g2 = len(g2hist)
                w2dv = b.dma("gpsimd", "dg", wq2b[:, half * 1024:(half + 1) * 1024],
                             wq2t[dffc],
                             waits=([("T", g2hist[g2 - 2])] if g2 >= 2
                                    else [("ds", w2_spill_done)]))

                def m2f(t, dffc=dffc, half=half, w2dv=w2dv,
                        gq=gq, vnow=b.cnt["V"]):
                    b.w(t, ("dg", w2dv), ("V", gq))
                    if dffc == 0:
                        b.w(t, ("V", vnow))
                    last = None
                    for tcrel in range(2):
                        pY = pBig[:, tcrel * 1024:(tcrel + 1) * 1024]
                        for s2 in range(2):
                            last = t.matmul(
                                pY[:, s2 * 512:(s2 + 1) * 512],
                                sqset[:, dffc * TG + tcrel * 128:dffc * TG + tcrel * 128 + 128],
                                wq2b[:, half * 1024 + s2 * 512:half * 1024 + (s2 + 1) * 512],
                                start=(dffc == 0), stop=(dffc == NF - 1))
                    return last
                g2hist.append(b.stepT(m2f))
            tvy = b.cnt["T"]
            for tcrel in range(2):
                tc = tg * 2 + tcrel
                pY = pBig[:, tcrel * 1024:(tcrel + 1) * 1024]

                def yev(v, tc=tc, tcrel=tcrel, pY=pY, tvy=tvy):
                    b.w(v, ("T", tvy))
                    xs = xbuf_c[:, tc, :]
                    return v.scalar_tensor_tensor(xs, pY[:], r2t[:, tcrel:tcrel + 1],
                                                  xs, ALU.mult, ALU.add)
                b.stepV(yev)

        _ntg = int(os.environ.get("KB_S2TG", NG))
        _nopipe = bool(os.environ.get("KB_S2NOPIPE"))
        emit_m1(0)
        for tg in range(_ntg):
            gq = emit_q2(tg)
            if not _nopipe and tg + 1 < _ntg:
                emit_m1(tg + 1)
            emit_m2(tg, gq)
            if _nopipe and tg + 1 < _ntg:
                emit_m1(tg + 1)

        if DBG:
            for nm, tt in (("dbg_mtv", mtv), ("dbg_dqt", dqt), ("dbg_shset", shset),
                           ("dbg_sqset", sqset), ("dbg_r2t", r2t), ("dbg_s2t", s2t),
                           ("dbg_Br", Br), ("dbg_Bs2", Bs2), ("dbg_gmax", gmax), ("dbg_m", dbgm)):
                ap = nc.dram_tensor(nm, [tt.shape[0], tt.shape[1]], tt.dtype,
                                    kind="ExternalOutput").ap()
                b.dma("sync", "do", ap, tt[:],
                      waits=[("V", b.cnt["V"]), ("S", b.cnt["S"]), ("T", b.cnt["T"])])
        stream_ln(xbuf)
        ck("s2ln")
        b.flush()


def emit_l1_attn(b, w, ck, env):
    sem = b.sem
    nc = b.nc
    xbuf, xbuf_c = env["xbuf"], env["xbuf_c"]
    fmA, wsb, btile, xcb = env["fmA"], env["wsb"], env["btile"], env["xcb"]
    ident, identf = env["ident"], env["identf"]
    st1, st2 = env["st1"], env["st2"]
    qch = env["qch"]
    pT, pS1, pW, pBig = env["pT"], env["pS1"], env["pW"], env["pBig"]
    cEPSL = env["cEPSL"]
    stream_ln, mm_acc = env["stream_ln"], env["mm_acc"]

    w1blk = nc.dram_tensor("w1blk_scr", [32, 128, 1024], BF16, kind="Internal").ap()
    w2blk = nc.dram_tensor("w2blk_scr", [32, 128, 1024], BF16, kind="Internal").ap()
    kfscr = nc.dram_tensor("kf_scr", [NT, 128, 1024], BF16, kind="Internal").ap()

    with ExitStack() as sc:
        b.autoflush = ("head" in AFR) or ("all" in AFR)
        scB = ExitStack()
        scC = ExitStack()
        zch = b.sbuf("zch", [128, D], F32, sc)
        e1 = env["qsq"]
        e2 = env["qch"]
        G1t = b.sbuf("G1t", [128, D], BF16, sc)
        BE1t = b.sbuf("BE1t", [128, D], BF16, sc)
        ebf = b.sbuf("ebf", [128, 2 * 512], BF16, sc)
        wcb = b.sbuf("wcb", [128, 2 * 1024], BF16, sc)
        gdv = b.dma("gpsimd", "dg", G1t[:],
                    w["L1_attn_g1"][None, :].broadcast_to([128, D]))
        bdv = b.dma("gpsimd", "dg", BE1t[:],
                    w["L1_attn_be1"][None, :].broadcast_to([128, D]))

        def ln_gb_transpose(src_c, gt, bt, gbdv):
            for tc in range(NT):
                ch = src_c[:, tc, :]

                def stats(v, ch=ch, tc=tc):
                    if tc == 0:
                        b.w(v, ("dg", gbdv))
                    v.bn_stats(st1[:, 0:6], ch[:, 0:512])
                    v.bn_stats(st1[:, 6:12], ch[:, 512:1024])
                    v.drain()
                    return v.bn_aggr(st2[:, 0:2], st1[:, 0:12])
                vv = b.stepV(stats)

                def sq(s, vv=vv):
                    b.w(s, ("V", vv))
                    return s.activation(st2[:, 2:3], st2[:, 1:2], AF.Sqrt,
                                        bias=cEPSL[:])
                sv = b.stepS(sq)

                def appl(v, ch=ch, sv=sv, tprev=b.cnt["T"]):
                    b.w(v, ("S", sv), ("T", tprev))
                    v.reciprocal(st2[:, 3:4], st2[:, 2:3])
                    v.drain()
                    v.tensor_scalar(st2[:, 4:5], st2[:, 0:1], st2[:, 3:4], -1.0,
                                    ALU.mult, ALU.mult)
                    v.drain()
                    v.tensor_scalar(zch[:], ch, st2[:, 3:4], st2[:, 4:5],
                                    ALU.mult, ALU.add)
                    v.tensor_tensor(zch[:], zch[:], gt[:], ALU.mult)
                    return v.tensor_tensor(xcb[:], zch[:], bt[:], ALU.add)
                vv2 = b.stepV(appl)

                def tp(t, vv2=vv2, sprev=b.cnt["S"]):
                    b.w(t, ("V", vv2), ("S", sprev))
                    for dc in range(7):
                        t.transpose(pT[:, dc * 128:(dc + 1) * 128],
                                    xcb[:, dc * 128:(dc + 1) * 128], ident[:])
                    return t.transpose(pT[:, 7 * 128:8 * 128],
                                       xcb[:, 7 * 128:8 * 128], ident[:])
                tv = b.stepT(tp)

                def evac(s, tc=tc, tv=tv):
                    b.w(s, ("T", tv))
                    for dc in range(7):
                        s.copy(fmA[:, dc * T + tc * 128:dc * T + (tc + 1) * 128],
                               pT[:, dc * 128:(dc + 1) * 128])
                    return s.copy(fmA[:, 7 * T + tc * 128:7 * T + (tc + 1) * 128],
                                  pT[:, 7 * 128:8 * 128])
                b.stepS(evac)

        ln_gb_transpose(xbuf_c, G1t, BE1t, b.cnt["dg"])
        ck("s3ln1")
        b.autoflush = ("qkvr" in AFR) or ("all" in AFR)

        qfT = b.sbuf("qfT", [128, NT * D], BF16, scC)
        kvb = b.sbuf("kvb", [128, 16 * 65], BF16, scC)
        kfc = b.sbuf("kfc", [128, 2 * 1024], BF16, scB)
        vA = b.sbuf("vA", [128, NT * 1040], BF16, scB)
        vA_c = vA[:].rearrange("p (c h f) -> p c h f", c=NT, h=16)

        kf_spill = [0] * NT

        def vmisc(v):
            return v.memset(vA_c[:, :, :, 64:65], 1.0)
        v_va = b.stepV(vmisc)

        qkv_w_ap = w["L1_attn_qkv_w"].rearrange("(c p) f -> p c f", p=128)
        af_qkv = b.autoflush
        for strip in range(6):
            if "strip" in AFR:
                b.autoflush = af_qkv or (("st%d" % strip) in AFR)
            c0 = strip * 512
            wdv = b.dma("gpsimd", "dg",
                        wsb[:, 0:ND * 512].rearrange("p (c f) -> p c f", c=ND),
                        qkv_w_ap[:, :, c0:c0 + 512], waits=[("T", b.cnt["T"])])
            bdv2 = b.dma("gpsimd", "dg", btile[:, 0:512],
                         w["L1_attn_qkv_b"][None, c0:c0 + 512].broadcast_to([128, 512]),
                         waits=[("V", b.cnt["V"])])
            for tc in range(NT):
                po = pS1[:, 0:512] if tc % 2 == 0 else pW[:, 0:512]
                prev_v = b.cnt["V"]
                prev_s = b.cnt["S"]

                def qmm(t, tc=tc, po=po, wdv=wdv, prev_v=prev_v, prev_s=prev_s):
                    b.w(t, ("dg", wdv), ("S", prev_s), ("V", prev_v))
                    return mm_acc(t, po,
                                  lambda dc, tc=tc: fmA[:, dc * T + tc * 128:dc * T + (tc + 1) * 128],
                                  lambda dc: wsb[:, dc * 512:(dc + 1) * 512], ND)
                tv = b.stepT(qmm)

                if strip < 4:
                    dc0 = (strip % 2) * 512
                    is_q = strip < 2

                    def elu_a(v, tv=tv, po=po):
                        b.w(v, ("T", tv), ("dg", bdv2))
                        v.tensor_tensor(e1[:], po, btile[:, 0:512], ALU.add)
                        v.drain()
                        return v.tensor_scalar(e2[:], e1[:], 0.0, None, ALU.min)
                    vv1 = b.stepV(elu_a)

                    def elu_b(s, vv1=vv1):
                        b.w(s, ("V", vv1))
                        return s.activation(e2[:], e2[:], AF.Exp)
                    sv1 = b.stepS(elu_b)

                    if is_q:
                        def elu_c(v, tc=tc, dc0=dc0, sv1=sv1):
                            b.w(s if False else v, ("S", sv1))
                            v.tensor_relu(e1[:], e1[:])
                            v.drain()
                            return v.tensor_tensor(
                                qfT[:, tc * D + dc0:tc * D + dc0 + 512],
                                e1[:], e2[:], ALU.add)
                        b.stepV(elu_c)
                    else:
                        eb = ebf[:, (tc % 2) * 512:(tc % 2) * 512 + 512]

                        def elu_ck(v, sv1=sv1, eb=eb, dprev=b.cnt["ds"]):
                            b.w(v, ("S", sv1), ("ds", dprev))
                            v.tensor_relu(e1[:], e1[:])
                            v.drain()
                            return v.tensor_tensor(eb, e1[:], e2[:], ALU.add)
                        vv3 = b.stepV(elu_ck)
                        kf_spill[tc] = b.dma("sync", "ds",
                                             kfscr[tc][:, dc0:dc0 + 512], eb,
                                             waits=[("V", vv3)])
                else:
                    h0 = (strip - 4) * 4

                    def vev(v, tc=tc, tv=tv, po=po, strip=strip):
                        b.w(v, ("T", tv), ("dg", bdv2))
                        return v.scalar_tensor_tensor(
                            vA_c[:, tc, (strip - 4) * 8:(strip - 4) * 8 + 8, 0:64],
                            po.rearrange("p (h f) -> p h f", h=8),
                            1.0,
                            btile[:, 0:512].rearrange("p (h f) -> p h f", h=8),
                            ALU.mult, ALU.add)
                    b.stepV(vev)
        ck("s3qkv")
        b.autoflush = ("kvr" in AFR) or ("all" in AFR)

        w1ap = w["L1_attn_w1"].rearrange("(c p) f -> p c f", p=128)
        w2ap = w["L1_attn_w2"].rearrange("(c p) f -> p c f", p=128)
        _pp = []
        for i in range(32):
            dc, piece = i // 4, i % 4
            half = i % 2
            dv = b.dma("gpsimd", "dg", wcb[:, half * 1024:(half + 1) * 1024],
                       w1ap[:, dc, piece * 1024:(piece + 1) * 1024],
                       waits=([("ds", _pp[-2])] if len(_pp) >= 2 else []))
            _pp.append(b.dma("sync", "ds",
                             w1blk.rearrange("c p f -> p c f", c=32)
                             [:, piece * 8:piece * 8 + 8, dc * 128:(dc + 1) * 128],
                             wcb[:, half * 1024:(half + 1) * 1024]
                             .rearrange("p (c f) -> p c f", c=8),
                             waits=[("dg", dv)]))
        for fc in range(32):
            half = fc % 2
            dv = b.dma("gpsimd", "dg", wcb[:, half * 1024:(half + 1) * 1024],
                       w2ap[:, fc, :],
                       waits=([("ds", _pp[-2])] if len(_pp) >= 2 else []))
            _pp.append(b.dma("sync", "ds", w2blk[fc],
                             wcb[:, half * 1024:(half + 1) * 1024],
                             waits=[("dg", dv)]))
        _spill_done_s3 = b.cnt["ds"]

        def kvz(v):
            return v.memset(kvb[:], 0.0)
        v_kvz = b.stepV(kvz)

        for kc in range(NT):
            half = kc % 2
            kdv = b.dma("sync", "ds", kfc[:, half * 1024:(half + 1) * 1024],
                        kfscr[kc],
                        waits=([("T", b.cnt["T"])] if kc >= 2 else []) +
                        [("ds", kf_spill[kc])])

            def kvmm(t, kc=kc, half=half, kdv=kdv, vprev=b.cnt["V"]):
                b.w(t, ("ds", kdv))
                if kc == 0:
                    b.w(t, ("V", vprev))
                last = None
                for h in range(H):
                    st_flag = (kc == 0 and h % 4 == 0)
                    last = t.matmul(pBig[0:64, h * 128:h * 128 + 65],
                                    kfc[:, half * 1024 + h * 64:half * 1024 + (h + 1) * 64],
                                    vA[:, kc * 1040 + h * 65:kc * 1040 + h * 65 + 65],
                                    start=st_flag,
                                    stop=(kc == NT - 1))
                return last
            b.stepT(kvmm)
        tv_kv = b.cnt["T"]

        def kv_ev(v, tv_kv=tv_kv):
            b.w(v, ("T", tv_kv))
            return v.tensor_copy(
                kvb[0:64, :].rearrange("p (h f) -> p h f", h=16),
                pBig[0:64, :].rearrange("p (h f) -> p h f", h=16)[:, :, 0:65])
        v_kv = b.stepV(kv_ev)
        b.flush()
        scB.close()
        ck("s3kv")
        b.autoflush = ("yreg" in AFR) or ("all" in AFR)

        yTM = b.sbuf("yTM", [128, NT * D], BF16, scC)
        qfA = b.sbuf("qfA", [128, T], BF16, scC)

        def qfz(v):
            return v.memset(qfA[:], 0.0)
        b.stepV(qfz)
        yev4_hist = []

        for h in range(H):
            for half in range(2):
                def tpq(t, h=h, half=half, sprev=b.cnt["S"], vprev=b.cnt["V"]):
                    b.w(t, ("S", sprev), ("V", vprev))
                    last = None
                    for j in range(8):
                        tc = half * 8 + j
                        last = t.transpose(pT[0:64, j * 128:(j + 1) * 128],
                                           qfT[:, tc * D + h * 64:tc * D + (h + 1) * 64],
                                           ident[:])
                    return last
                tvq = b.stepT(tpq)

                def evq(s, half=half, tvq=tvq):
                    b.w(s, ("T", tvq))
                    return s.copy(qfA[0:64, half * 1024:(half + 1) * 1024],
                                  pT[0:64, 0:1024])
                b.stepS(evq)
            sv_q = b.cnt["S"]
            for grp in range(NT // 4):
                gi = len(yev4_hist) + 0
                buf = pS1 if grp % 2 == 0 else pW
                last_tvy = None
                for sl in range(4):
                    tc = grp * 4 + sl

                    def ymm(t, h=h, tc=tc, sl=sl, buf=buf, sv_q=sv_q,
                            v_kv=v_kv,
                            vprev=(yev4_hist[-2] if len(yev4_hist) >= 2
                                   else b.cnt["V"])):
                        if sl == 0:
                            b.w(t, ("S", sv_q), ("V", max(v_kv, vprev)))
                        return t.matmul(buf[:, sl * 65:sl * 65 + 65],
                                        qfA[:, tc * 128:(tc + 1) * 128],
                                        kvb[:, h * 65:(h + 1) * 65],
                                        start=True, stop=True)
                    last_tvy = b.stepT(ymm)

                def yev4(v, h=h, grp=grp, buf=buf, tvy=last_tvy):
                    b.w(v, ("T", tvy))
                    bufr = buf[:, 0:260].rearrange("p (s f) -> p s f", s=4)
                    v.tensor_scalar_max(st1[:, 0:4], bufr[:, :, 64], 1e-6)
                    v.drain()
                    v.reciprocal(st1[:, 4:8], st1[:, 0:4])
                    v.drain()
                    last = None
                    for sl in range(4):
                        tc = grp * 4 + sl
                        last = v.tensor_scalar(
                            yTM[:, tc * D + h * 64:tc * D + (h + 1) * 64],
                            buf[:, sl * 65:sl * 65 + 64],
                            st1[:, 4 + sl:5 + sl], None, ALU.mult)
                    return last
                yev4_hist.append(b.stepV(yev4))
        ck("s3y")

        def y_transpose():
            for tc in range(NT):
                def tp(t, tc=tc, sprev=b.cnt["S"], vprev=b.cnt["V"]):
                    b.w(t, ("S", sprev), ("V", vprev))
                    last = None
                    for dc in range(ND):
                        last = t.transpose(pT[:, dc * 128:(dc + 1) * 128],
                                           yTM[:, tc * D + dc * 128:tc * D + (dc + 1) * 128],
                                           ident[:])
                    return last
                tv = b.stepT(tp)

                def evac(s, tc=tc, tv=tv):
                    b.w(s, ("T", tv))
                    last = None
                    for dc in range(ND):
                        last = s.copy(fmA[:, dc * T + tc * 128:dc * T + (tc + 1) * 128],
                                      pT[:, dc * 128:(dc + 1) * 128])
                    return last
                b.stepS(evac)
        y_transpose()
        b.barrier()
        scC.close()
        b.autoflush = ("outp" in AFR) or ("all" in AFR)
        xx = b.sbuf("xx", [128, NT * D], F32, sc)
        xx_c = xx[:].rearrange("p (c f) -> p c f", c=NT)

        ow_ap = w["L1_attn_out_w"].rearrange("(c p) f -> p c f", p=128)
        for strip in range(2):
            c0 = strip * 512
            wdv = b.dma("gpsimd", "dg",
                        wsb[:, 0:ND * 512].rearrange("p (c f) -> p c f", c=ND),
                        ow_ap[:, :, c0:c0 + 512], waits=[("T", b.cnt["T"])])
            bdv3 = b.dma("gpsimd", "dg", btile[:, 0:512],
                         w["L1_attn_out_b"][None, c0:c0 + 512].broadcast_to([128, 512]),
                         waits=[("V", b.cnt["V"])])
            for tc in range(NT):
                po = pS1[:, 0:512] if tc % 2 == 0 else pW[:, 0:512]
                prev_v = b.cnt["V"]

                def omm(t, tc=tc, po=po, wdv=wdv, prev_v=prev_v, sprev=b.cnt["S"]):
                    b.w(t, ("dg", wdv), ("S", sprev), ("V", prev_v))
                    return mm_acc(t, po,
                                  lambda dc, tc=tc: fmA[:, dc * T + tc * 128:dc * T + (tc + 1) * 128],
                                  lambda dc: wsb[:, dc * 512:(dc + 1) * 512], ND)
                tv = b.stepT(omm)

                def oev(v, tc=tc, tv=tv, po=po, c0=c0):
                    b.w(v, ("T", tv), ("dg", bdv3))
                    v.tensor_tensor(qch[:], po, btile[:, 0:512], ALU.add)
                    v.drain()
                    return v.tensor_tensor(xx_c[:, tc, c0:c0 + 512],
                                           xbuf_c[:, tc, c0:c0 + 512],
                                           qch[:], ALU.add)
                b.stepV(oev)
        ck("s3out")
        b.autoflush = b.autoflush or ("ln2" in AFR)

        gdv2 = b.dma("gpsimd", "dg", G1t[:],
                     w["L1_attn_g2"][None, :].broadcast_to([128, D]),
                     waits=[("V", b.cnt["V"])])
        bdv4 = b.dma("gpsimd", "dg", BE1t[:],
                     w["L1_attn_be2"][None, :].broadcast_to([128, D]),
                     waits=[("V", b.cnt["V"])])
        ln_gb_transpose(xx_c, G1t, BE1t, b.cnt["dg"])
        ck("s3ln2")
        b.autoflush = ("mlp" in AFR) or ("all" in AFR)

        b1t = b.sbuf("b1t", [128, 32], F32, sc)
        B2t = b.sbuf("B2t", [128, D], F32, sc)
        hmb = b.sbuf("hmb", [128, 2 * 256], BF16, sc)
        w1b = wsb
        w2b = b.sbuf("w2b", [128, 4 * 1024], BF16, sc)
        b1dv = b.dma("sync", "ds", b1t[:],
                     w["L1_attn_b1"].rearrange("(c p) -> p c", p=128), slow=True)
        b2dv = b.dma("gpsimd", "dg", B2t[:],
                     w["L1_attn_b2"][None, :].broadcast_to([128, D]))
        h1_mlp, h2_mlp = [], []
        mlp_spill_done = _spill_done_s3
        for tg in range(8):
            t0 = tg * 256
            for fc in range(32):
                half = fc % 2
                g = len(h1_mlp)
                qtr = g % 4
                pm = pS1[:, 0:256] if half == 0 else pW[:, 0:256]
                hv = hmb[:, half * 256:(half + 1) * 256]
                w1dv = b.dma("sync", "ds", w1b[:, qtr * 1024:(qtr + 1) * 1024],
                             w1blk[fc],
                             waits=[("T", h1_mlp[g - 4])] if g >= 4
                             else [("T", b.cnt["T"]), ("ds", mlp_spill_done)])
                w2dv = b.dma("gpsimd", "dg", w2b[:, qtr * 1024:(qtr + 1) * 1024],
                             w2blk[fc],
                             waits=[("T", h2_mlp[g - 4])] if g >= 4
                             else [("ds", mlp_spill_done)])

                def m1f(t, fc=fc, qtr=qtr, pm=pm, w1dv=w1dv, t0=t0,
                        vprev=b.cnt["V"], sprev=b.cnt["S"]):
                    b.w(t, ("ds", w1dv), ("V", vprev), ("S", sprev))
                    return mm_acc(t, pm,
                                  lambda dc, qtr=qtr: w1b[:, qtr * 1024 + dc * 128:qtr * 1024 + (dc + 1) * 128],
                                  lambda dc, t0=t0: fmA[:, dc * T + t0:dc * T + t0 + 256],
                                  ND)
                tv1 = b.stepT(m1f)
                h1_mlp.append(tv1)

                def gel(s, fc=fc, tv1=tv1, pm=pm, hv=hv, b1dv=b1dv):
                    b.w(s, ("T", tv1), ("ds", b1dv))
                    return s.activation(hv, pm, AF.Gelu, bias=b1t[:, fc:fc + 1])
                sv1 = b.stepS(gel)

                def m2f(t, fc=fc, qtr=qtr, tcref=tg, sv1=sv1, w2dv=w2dv, hv=hv):
                    b.w(t, ("S", sv1), ("dg", w2dv))
                    last = None
                    for tcrel in range(2):
                        pY = pBig[:, tcrel * 1024:(tcrel + 1) * 1024]
                        for s2 in range(2):
                            last = t.matmul(
                                pY[:, s2 * 512:(s2 + 1) * 512],
                                hv[:, tcrel * 128:(tcrel + 1) * 128],
                                w2b[:, qtr * 1024 + s2 * 512:qtr * 1024 + (s2 + 1) * 512],
                                start=(fc == 0), stop=(fc == 31))
                    return last
                h2_mlp.append(b.stepT(m2f))
            tvy2 = b.cnt["T"]

            for tcrel in range(2):
                tc = tg * 2 + tcrel
                pY = pBig[:, tcrel * 1024:(tcrel + 1) * 1024]

                def fev(v, tc=tc, pY=pY, tvy2=tvy2, b2dv=b2dv):
                    b.w(v, ("T", tvy2), ("dg", b2dv))
                    v.tensor_tensor(zch[:], pY, B2t[:], ALU.add)
                    v.drain()
                    v.tensor_tensor(zch[:], zch[:], xx_c[:, tc, :], ALU.add)
                    v.drain()
                    return v.tensor_tensor(xbuf_c[:, tc, :], xbuf_c[:, tc, :],
                                           zch[:], ALU.add)
                b.stepV(fev)
        b.autoflush = ("sln" in AFR) or ("all" in AFR)
        stream_ln(xbuf)
        b.flush()
        b.autoflush = "all" in AFR


def emit_l1_ffn(b, w, ck, env):
    sem = b.sem
    nc = b.nc
    xbuf, xbuf_c = env["xbuf"], env["xbuf_c"]
    fmA, xcb = env["fmA"], env["xcb"]
    ident = env["ident"]
    pT, pS1, pW, pBig = env["pT"], env["pS1"], env["pW"], env["pBig"]
    stream_ln, mm_acc = env["stream_ln"], env["mm_acc"]
    cast_transpose = env["cast_transpose"]
    S127 = 1.0 / 127.0
    NFC = 24

    wgt = nc.dram_tensor("wgt_scr", [NFC, 128, 1024], BF16, kind="Internal").ap()
    wut = nc.dram_tensor("wut_scr", [NFC, 128, 1024], BF16, kind="Internal").ap()
    wdt = nc.dram_tensor("wdt_scr", [NFC, 128, 1024], BF16, kind="Internal").ap()

    with ExitStack() as sc:
        wcb = b.sbuf("wcb4", [128, 2 * 1024], BF16, sc)
        ttile = b.sbuf("ttile4", [128, 2 * 1024], BF16, sc)
        sgb = b.sbuf("sgb", [128, 256], F32, sc)
        hmb = b.sbuf("hmb4", [128, 2 * 256], BF16, sc)
        wgb = b.sbuf("wgb", [128, 4 * 1024], BF16, sc)
        wub = b.sbuf("wub", [128, 4 * 1024], BF16, sc)
        wdb = b.sbuf("wdb", [128, 4 * 1024], BF16, sc)

        cast_transpose(lambda tc: xbuf_c[:, tc, :], fmA, ND,
                       waits=(("V", b.cnt["V"]),))

        def prep_t(src_ap_of, n, dst_of, st):
            t_hist, s_hist, d_hist = st["t"], st["s"], st["d"]
            for i in range(n):
                g = len(t_hist)
                half = g % 2
                dv = b.dma("gpsimd", "dg", wcb[:, half * 1024:(half + 1) * 1024],
                           src_ap_of(i),
                           waits=[("T", t_hist[g - 2])] if g >= 2
                           else [("S", b.cnt["S"])])

                def tpp(t, dv=dv, half=half,
                        sprev=(s_hist[g - 1] if g >= 1 else b.cnt["S"])):
                    b.w(t, ("dg", dv), ("S", sprev))
                    last = None
                    for dc in range(8):
                        last = t.transpose(pT[:, dc * 128:(dc + 1) * 128],
                                           wcb[:, half * 1024 + dc * 128:half * 1024 + (dc + 1) * 128],
                                           ident[:])
                    return last
                tv = b.stepT(tpp)
                t_hist.append(tv)

                def evt(s, tv=tv, half=half,
                        dprev=(d_hist[g - 2] if g >= 2 else b.cnt["ds"])):
                    b.w(s, ("T", tv), ("ds", dprev))
                    for dc in range(7):
                        s.copy(ttile[:, half * 1024 + dc * 128:half * 1024 + (dc + 1) * 128],
                               pT[:, dc * 128:(dc + 1) * 128])
                    return s.copy(ttile[:, half * 1024 + 7 * 128:half * 1024 + 8 * 128],
                                  pT[:, 7 * 128:8 * 128])
                sv = b.stepS(evt)
                s_hist.append(sv)
                d_hist.append(b.dma("sync", "ds", dst_of(i),
                                    ttile[:, half * 1024:(half + 1) * 1024],
                                    waits=[("S", sv)]))

        wg_ap = w["L1_ffn_w_gate"].rearrange("(c p) f -> p c f", p=128)
        wu_ap = w["L1_ffn_w_up"].rearrange("(c p) f -> p c f", p=128)
        wd_ap = w["L1_ffn_w_down"].rearrange("(c p) f -> p c f", p=128)
        _pst = {"t": [], "s": [], "d": []}
        prep_t(lambda i: wg_ap[:, i, :], NFC, lambda i: wgt[i], _pst)
        prep_t(lambda i: wu_ap[:, i, :], NFC, lambda i: wut[i], _pst)
        prep_t(lambda i: wd_ap[:, i // 3, (i % 3) * 1024:(i % 3 + 1) * 1024], NFC,
               lambda i: wdt.rearrange("c p f -> p c f", c=NFC)
               [:, (i % 3) * 8:(i % 3) * 8 + 8, (i // 3) * 128:(i // 3 + 1) * 128],
               _pst)
        ck("s4prep")

        gu_hist, d_hist2 = [], []
        prep_done = b.cnt["ds"]
        for tg in range(8):
            t0 = tg * 256
            for fc in range(NFC):
                g = len(gu_hist)
                qtr = g % 4
                half = fc % 2
                gdv = b.dma("sync", "ds", wgb[:, qtr * 1024:(qtr + 1) * 1024],
                            wgt[fc],
                            waits=[("T", gu_hist[g - 4])] if g >= 4
                            else [("ds", prep_done)])
                udv = b.dma("gpsimd", "dg", wub[:, qtr * 1024:(qtr + 1) * 1024],
                            wut[fc],
                            waits=[("T", gu_hist[g - 4])] if g >= 4
                            else [("ds", prep_done)])
                ddv = b.dma("sync", "ds", wdb[:, qtr * 1024:(qtr + 1) * 1024],
                            wdt[fc],
                            waits=[("T", d_hist2[g - 4])] if g >= 4
                            else [("ds", prep_done)])

                def gumm(t, fc=fc, qtr=qtr, t0=t0, gdv=gdv, udv=udv,
                         sprev=b.cnt["S"], vprev=b.cnt["V"]):
                    b.w(t, ("ds", gdv), ("dg", udv), ("S", sprev), ("V", vprev))
                    mm_acc(t, pS1[:, 0:256],
                           lambda dc, qtr=qtr: wgb[:, qtr * 1024 + dc * 128:qtr * 1024 + (dc + 1) * 128],
                           lambda dc, t0=t0: fmA[:, dc * T + t0:dc * T + t0 + 256],
                           ND)
                    return mm_acc(t, pW[:, 0:256],
                                  lambda dc, qtr=qtr: wub[:, qtr * 1024 + dc * 128:qtr * 1024 + (dc + 1) * 128],
                                  lambda dc, t0=t0: fmA[:, dc * T + t0:dc * T + t0 + 256],
                                  ND)
                tv1 = b.stepT(gumm)
                gu_hist.append(tv1)

                def sil(s, tv1=tv1):
                    b.w(s, ("T", tv1))
                    return s.activation(sgb[:], pS1[:, 0:256], AF.Silu, scale=S127)
                sv1 = b.stepS(sil)

                def hev(v, half=half, sv1=sv1):
                    b.w(v, ("S", sv1))
                    return v.scalar_tensor_tensor(
                        hmb[:, half * 256:(half + 1) * 256],
                        pW[:, 0:256], S127, sgb[:], ALU.mult, ALU.mult)
                vv1 = b.stepV(hev)

                def dmm(t, fc=fc, half=half, qtr=qtr, vv1=vv1, ddv=ddv):
                    b.w(t, ("V", vv1), ("ds", ddv))
                    last = None
                    for tcrel in range(2):
                        pY = pBig[:, tcrel * 1024:(tcrel + 1) * 1024]
                        for s2 in range(2):
                            last = t.matmul(
                                pY[:, s2 * 512:(s2 + 1) * 512],
                                hmb[:, half * 256 + tcrel * 128:half * 256 + tcrel * 128 + 128],
                                wdb[:, qtr * 1024 + s2 * 512:qtr * 1024 + (s2 + 1) * 512],
                                start=(fc == 0), stop=(fc == NFC - 1))
                    return last
                d_hist2.append(b.stepT(dmm))
            tvy = b.cnt["T"]

            for tcrel in range(2):
                tc = tg * 2 + tcrel
                pY = pBig[:, tcrel * 1024:(tcrel + 1) * 1024]

                def fev(v, tc=tc, pY=pY, tvy=tvy):
                    b.w(v, ("T", tvy))
                    return v.scalar_tensor_tensor(xbuf_c[:, tc, :], pY, S127,
                                                  xbuf_c[:, tc, :],
                                                  ALU.mult, ALU.add)
                b.stepV(fev)
        stream_ln(xbuf)
        b.flush()


from concourse.bass_utils import run_bass_kernel_spmd

_WSHAPES = {
    "L0_attn_Wq": ((1024, 1040), F32), "L0_attn_bq": ((1040,), F32),
    "L0_attn_Wk": ((1024, 1024), F32), "L0_attn_bk": ((1024,), F32),
    "L0_attn_Wv": ((1024, 1024), F32), "L0_attn_bv": ((1024,), F32),
    "L0_attn_Wo": ((1024, 1024), F32), "L0_attn_bo": ((1024,), F32),
    "L0_attn_qn_w": ((64,), F32), "L0_attn_kn_w": ((64,), F32),
    "L0_ffn_w1": ((4096, 1024), F32), "L0_ffn_conv_w": ((4096, 1, 3), F32),
    "L0_ffn_conv_b": ((4096,), F32), "L0_ffn_w2": ((1024, 4096), F32),
    "L1_attn_qkv_w": ((1024, 3072), F32), "L1_attn_qkv_b": ((3072,), F32),
    "L1_attn_out_w": ((1024, 1024), F32), "L1_attn_out_b": ((1024,), F32),
    "L1_attn_w1": ((1024, 4096), F32), "L1_attn_b1": ((4096,), F32),
    "L1_attn_w2": ((4096, 1024), F32), "L1_attn_b2": ((1024,), F32),
    "L1_attn_g1": ((1024,), F32), "L1_attn_be1": ((1024,), F32),
    "L1_attn_g2": ((1024,), F32), "L1_attn_be2": ((1024,), F32),
    "L1_ffn_w_gate": ((3072, 1024), I32), "L1_ffn_w_up": ((3072, 1024), I32),
    "L1_ffn_w_down": ((1024, 3072), I32),
}


def _build_nc():
    nc = bass.Bass("TRN2", target_bir_lowering=False)
    x = nc.dram_tensor("x", [2048, 1024], F32, kind="ExternalInput").ap()
    w = {}
    for k, (shp, dt) in _WSHAPES.items():
        w[k] = nc.dram_tensor(k, list(shp), dt, kind="ExternalInput").ap()
    w["cident"] = nc.dram_tensor("cident", [128, 128], BF16,
                                 kind="ExternalInput").ap()
    out = nc.dram_tensor("out", [2048, 1024], F32, kind="ExternalOutput").ap()
    build(nc, x, w, out, n_stages=4)
    return nc


def kernel(**inputs):
    xs = np.ascontiguousarray(np.asarray(inputs["L0_attn_x"], dtype=np.float32))
    nc = _build_nc()
    base = {k: np.ascontiguousarray(np.asarray(inputs[k])) for k in _WSHAPES}
    base["cident"] = np.eye(128, dtype=ml_dtypes.bfloat16)
    in_maps = [dict(base, x=xs[i]) for i in range(8)]
    res = run_bass_kernel_spmd(nc, in_maps, core_ids=list(range(8)))
    return np.stack([np.asarray(r["out"], dtype=np.float32)
                     for r in res.results], axis=0)
